# version 10
# baseline (speedup 1.0000x reference)
import numpy as np
from contextlib import ExitStack
import concourse.bass as bass
import concourse.mybir as mybir
from concourse.bass_utils import run_bass_kernel_spmd

F32 = mybir.dt.float32
F32R = mybir.dt.float32r
ALU = mybir.AluOpType
AF = mybir.ActivationFunctionType
AX = mybir.AxisListType

D = 2048
HID = 2816
FF = 8192
TSEQ = 8192
NCORES = 8
T_OWN = TSEQ // NCORES
HALO = 512
WARM = 256
T_EXT = T_OWN + HALO
T_SCAN = T_EXT + WARM
NF = D // 128
NH = HID // 128
NQ = FF // 128
HEADS, KVH, DH = 16, 4, 128
G = HEADS // KVH
WINDOW = 512
KSPAN = WINDOW + 128
NQT = T_OWN // 128
NKT = T_EXT // 128
EPS_LN = 1e-5
EPS_RMS = 1e-6

RUN_A = True
RUN_B = True
RUN_C = True
RUN_D = True

ENGS = ("tensor", "vector", "scalar", "gpsimd", "sync")


class Dep:
    __slots__ = ("eng", "cnt", "dma")

    def __init__(self, eng, cnt, dma=False):
        self.eng, self.cnt, self.dma = eng, cnt, dma


class Prog:
    def __init__(self):
        self.ops = {e: [] for e in ENGS}
        self.cnt = {e: 0 for e in ENGS}
        self.cntd = {e: 0 for e in ENGS}
        self.hwm = {e: {} for e in ENGS}

    def wait(self, eng, dep):
        if dep is None:
            return
        tgt = self.cntd[dep.eng] if dep.dma else dep.cnt
        key = (dep.eng, dep.dma)
        if not dep.dma and dep.eng == eng:
            assert tgt <= self.cnt[eng], "same-engine forward wait"
        elif self.hwm[eng].get(key, 0) >= tgt and not (dep.eng == eng and not dep.dma):
            return
        if self.hwm[eng].get(key, 0) >= tgt and dep.eng != eng:
            return
        self.ops[eng].append(("wait", dep.eng, dep.dma, tgt))
        self.hwm[eng][key] = max(self.hwm[eng].get(key, 0), tgt)

    def emit(self, eng, fn, deps=(), inc=1):
        for d in deps:
            self.wait(eng, d)
        self.ops[eng].append(("op", fn, inc))
        if inc == 16:
            self.cntd[eng] += 16
        else:
            self.cnt[eng] += inc
        return Dep(eng, self.cnt[eng])

    def dma(self, eng, out_ap, in_ap, deps=()):
        self.emit(eng, (lambda e, o=out_ap, i=in_ap: e.dma_start(out=o, in_=i)),
                  deps=deps, inc=16)
        return Dep(eng, self.cntd[eng], dma=True)

    def now(self, eng):
        return Dep(eng, self.cnt[eng])

    def nowd(self, eng):
        return Dep(eng, self.cntd[eng], dma=True)


class Arena:
    def __init__(self, width):
        self.width = width
        self.free = [(0, width)]
        self.live = {}

    def alloc(self, name, w, align=8):
        assert name not in self.live
        for i, (o, fw) in enumerate(self.free):
            oo = (o + align - 1) // align * align
            if oo + w <= o + fw:
                repl = []
                if oo > o:
                    repl.append((o, oo - o))
                if o + fw > oo + w:
                    repl.append((oo + w, o + fw - (oo + w)))
                self.free[i:i + 1] = repl
                self.live[name] = (oo, w)
                return oo
        raise MemoryError(f"arena OOM {name} w={w} live={sorted(self.live.items())}")

    def freev(self, name):
        o, w = self.live.pop(name)
        self.free.append((o, w))
        self.free.sort()
        m = []
        for a, b in self.free:
            if m and m[-1][0] + m[-1][1] == a:
                m[-1] = (m[-1][0], m[-1][1] + b)
            else:
                m.append((a, b))
        self.free = m


def build_program():
    nc = bass.Bass("TRN2", target_bir_lowering=False)
    P = Prog()
    dt = nc.dram_tensor

    xT = dt("xT", [D, T_SCAN], F32, kind="ExternalInput").ap()
    xpT = dt("xpT", [D, T_SCAN], F32, kind="ExternalInput").ap()
    Wk0 = dt("Wk0", [HID, D], F32, kind="ExternalInput").ap()
    Wr0 = dt("Wr0", [HID, D], F32, kind="ExternalInput").ap()
    Wv0 = dt("Wv0", [D, HID], F32, kind="ExternalInput").ap()
    Bup = dt("Bup", [FF, D], F32, kind="ExternalInput").ap()
    Bdn = dt("Bdn", [D, FF], F32, kind="ExternalInput").ap()
    Wq1 = dt("Wq1", [D, D], F32, kind="ExternalInput").ap()
    Wk1 = dt("Wk1", [KVH * DH, D], F32, kind="ExternalInput").ap()
    Wv1 = dt("Wv1", [KVH * DH, D], F32, kind="ExternalInput").ap()
    Wo1 = dt("Wo1", [D, D], F32, kind="ExternalInput").ap()
    Uup = dt("Uup", [FF, D], F32, kind="ExternalInput").ap()
    Ug = dt("Ug", [FF, D], F32, kind="ExternalInput").ap()
    Udn = dt("Udn", [D, FF], F32, kind="ExternalInput").ap()
    v_tmk = dt("v_tmk", [128, NF], F32, kind="ExternalInput").ap()
    v_tmr = dt("v_tmr", [128, NF], F32, kind="ExternalInput").ap()
    v_td = dt("v_td", [128, NH], F32, kind="ExternalInput").ap()
    v_g0 = dt("v_g0", [128, NF], F32, kind="ExternalInput").ap()
    v_b0 = dt("v_b0", [128, NF], F32, kind="ExternalInput").ap()
    v_us0 = dt("v_us0", [128, NQ], F32, kind="ExternalInput").ap()
    v_ds0 = dt("v_ds0", [128, NF], F32, kind="ExternalInput").ap()
    v_os0 = dt("v_os0", [1, 1], F32, kind="ExternalInput").ap()
    v_sug = dt("v_sug", [128, NF], F32, kind="ExternalInput").ap()
    v_sdp = dt("v_sdp", [128, NQ], F32, kind="ExternalInput").ap()
    v_wsu = dt("v_wsu", [128, NQ], F32, kind="ExternalInput").ap()
    v_wsg = dt("v_wsg", [128, NQ], F32, kind="ExternalInput").ap()
    v_wsd = dt("v_wsd", [128, NF], F32, kind="ExternalInput").ap()
    v_w1q = dt("v_w1q", [128, 1], F32, kind="ExternalInput").ap()
    v_w1k = dt("v_w1k", [128, 1], F32, kind="ExternalInput").ap()
    c_ident = dt("c_ident", [128, 128], F32, kind="ExternalInput").ap()
    c_h128 = dt("c_h128", [128, 128], F32, kind="ExternalInput").ap()
    c_rot = dt("c_rot", [128, 128], F32, kind="ExternalInput").ap()
    c_ones1 = dt("c_ones1", [1, 128], F32, kind="ExternalInput").ap()
    c_cosqT = dt("c_cosqT", [DH, T_OWN], F32, kind="ExternalInput").ap()
    c_sinqT = dt("c_sinqT", [DH, T_OWN], F32, kind="ExternalInput").ap()
    c_coskT = dt("c_coskT", [DH, T_EXT], F32, kind="ExternalInput").ap()
    c_sinkT = dt("c_sinkT", [DH, T_EXT], F32, kind="ExternalInput").ap()
    c_mask = dt("c_mask", [NQT, 128, KSPAN], F32, kind="ExternalInput").ap()

    WkT_s = dt("WkT_s", [NH, 128, D], F32, kind="Internal").ap()
    WrT_s = dt("WrT_s", [NH, 128, D], F32, kind="Internal").ap()
    WvT_s = dt("WvT_s", [NF, 128, HID], F32, kind="Internal").ap()
    BupT_s = dt("BupT_s", [NQ, 128, D], F32, kind="Internal").ap()
    BdnT_s = dt("BdnT_s", [NF, 128, FF], F32, kind="Internal").ap()
    WqT_s = dt("WqT_s", [NF, 128, D], F32, kind="Internal").ap()
    WkT1_s = dt("WkT1_s", [KVH, 128, D], F32, kind="Internal").ap()
    WvT1_s = dt("WvT1_s", [KVH, 128, D], F32, kind="Internal").ap()
    WoT_s = dt("WoT_s", [NF, 128, D], F32, kind="Internal").ap()
    UupT_s = dt("UupT_s", [NQ, 128, D], F32, kind="Internal").ap()
    UgT_s = dt("UgT_s", [NQ, 128, D], F32, kind="Internal").ap()
    UdnT_s = dt("UdnT_s", [NF, 128, FF], F32, kind="Internal").ap()

    x1_s = dt("x1_s", [D, T_EXT], F32, kind="ExternalOutput").ap()
    x2_s = dt("x2_s", [D, T_EXT], F32, kind="ExternalOutput").ap()
    x3_s = dt("x3_s", [D, T_OWN], F32, kind="ExternalOutput").ap()
    x4T = dt("x4T", [D, T_OWN], F32, kind="ExternalOutput").ap()

    es = ExitStack()
    with es:
        AR_W = 50900
        ARF_W = 104
        ar_t = es.enter_context(nc.sbuf_tensor("arena", [128, AR_W], F32))
        arr_t = es.enter_context(nc.sbuf_tensor("arenar", [128, ARF_W], F32R))
        ar = Arena(AR_W)
        arf = Arena(ARF_W)
        sm_t = es.enter_context(nc.sbuf_tensor("smallar", [128, 1600], F32))
        small = Arena(1600)
        ones1 = es.enter_context(nc.sbuf_tensor("ones1", [1, 128], F32))
        ps_t = [es.enter_context(nc.psum_tensor(f"ps{i}", [128, 512], F32)) for i in range(8)]
        sems = {e: es.enter_context(nc.semaphore(f"sem_{e}")) for e in ENGS}
        semd = {e: es.enter_context(nc.semaphore(f"semd_{e}")) for e in ENGS}

        def A(off, w, p0=0, pn=128):
            return ar_t[p0:p0 + pn, off:off + w]

        def AR(off, w, p0=0, pn=128):
            return arr_t[p0:p0 + pn, off:off + w]

        def AFR(off, w, p0=0, pn=128):
            return arr_t[p0:p0 + pn, off:off + w].bitcast(F32)

        class Bank:
            def __init__(self):
                self.free_dep = [None] * 8
                self.rr = 0

            def acq(self):
                b = self.rr
                self.rr = (self.rr + 1) % 8
                return b, self.free_dep[b]

            def rel(self, b, dep):
                self.free_dep[b] = dep

        psum = Bank()

        def E(eng, method, *args, deps=(), inc=1, **kw):
            a = tuple(args)
            k = dict(kw)
            return P.emit(eng, (lambda e, m=method, a=a, k=k: getattr(e, m)(*a, **k)),
                          deps=deps, inc=inc)

        def TN():
            return P.now("tensor")

        for name, w in [("tmk", NF), ("tmk1", NF), ("tmr", NF), ("tmr1", NF),
                        ("wdec", NH), ("g0", NF), ("b0", NF), ("us0", NQ),
                        ("ds0", NF), ("os0", 1), ("sug", NF), ("sdp", NQ),
                        ("usg", NQ), ("gsg", NQ), ("uscomb", NQ), ("wsd", NF),
                        ("state", NH), ("ident", 128), ("h128", 128),
                        ("rot", 128), ("w1q", 1), ("w1k", 1), ("onescol", 1),
                        ("tstat", 8), ("dsg", 256)]:
            small.alloc(name, w)

        def SV(name, p0=0, pn=128, w=None):
            o, ww = small.live[name]
            return sm_t[p0:p0 + pn, o:o + (ww if w is None else w)]

        def STAT(i):
            o = small.live["tstat"][0]
            return sm_t[0:128, o + i:o + i + 1]

        def bar():
            return [P.now("tensor"), P.now("vector"), P.now("scalar"),
                    P.now("gpsimd"), P.nowd("scalar"), P.nowd("gpsimd")]


        for nm, src in [("tmk", v_tmk), ("tmr", v_tmr), ("wdec", v_td),
                        ("g0", v_g0), ("b0", v_b0), ("us0", v_us0),
                        ("ds0", v_ds0), ("sug", v_sug), ("sdp", v_sdp),
                        ("usg", v_wsu), ("gsg", v_wsg), ("wsd", v_wsd),
                        ("ident", c_ident), ("h128", c_h128), ("rot", c_rot),
                        ("w1q", v_w1q), ("w1k", v_w1k)]:
            P.dma("sync", SV(nm), src)
        P.dma("sync", SV("os0", 0, 1), v_os0)
        P.dma("sync", ones1[:, :], c_ones1)
        p_mask = ar.alloc("p_mask", NQT * KSPAN)
        p_cosq = ar.alloc("p_cosq", T_OWN)
        p_sinq = ar.alloc("p_sinq", T_OWN)
        p_cosk = ar.alloc("p_cosk", T_EXT)
        p_sink = ar.alloc("p_sink", T_EXT)
        for qt in range(NQT):
            P.dma("sync", A(p_mask + qt * KSPAN, KSPAN), c_mask[qt, :, :])
        P.dma("sync", A(p_cosq, T_OWN), c_cosqT[:, :])
        P.dma("sync", A(p_sinq, T_OWN), c_sinqT[:, :])
        P.dma("sync", A(p_cosk, T_EXT), c_coskT[:, :])
        P.dma("sync", A(p_sink, T_EXT), c_sinkT[:, :])
        dld = P.nowd("sync")

        dv = E("vector", "tensor_scalar", out=SV("tmk1"), in0=SV("tmk"),
               scalar1=-1.0, scalar2=1.0, op0=ALU.mult, op1=ALU.add, deps=[dld])
        dv = E("vector", "tensor_scalar", out=SV("tmr1"), in0=SV("tmr"),
               scalar1=-1.0, scalar2=1.0, op0=ALU.mult, op1=ALU.add, deps=[dv])
        dv = E("vector", "tensor_scalar", out=SV("sug"), in0=SV("sug"),
               scalar1=1.0 / (64.0 * float(np.sqrt(D))), scalar2=None,
               op0=ALU.mult, deps=[dv])
        dv = E("vector", "tensor_scalar", out=SV("sdp"), in0=SV("sdp"),
               scalar1=1.0 / (64.0 * float(np.sqrt(FF))), scalar2=None,
               op0=ALU.mult, deps=[dv])
        dv = E("vector", "tensor_scalar", out=SV("usg"), in0=SV("usg"),
               scalar1=64.0, scalar2=None, op0=ALU.mult, deps=[dv])
        dv = E("vector", "tensor_scalar", out=SV("gsg"), in0=SV("gsg"),
               scalar1=64.0, scalar2=None, op0=ALU.mult, deps=[dv])
        dv = E("vector", "tensor_tensor", out=SV("uscomb"), in0=SV("usg"),
               in1=SV("sdp"), op=ALU.mult, deps=[dv])
        dv = E("vector", "tensor_scalar", out=SV("wsd"), in0=SV("wsd"),
               scalar1=64.0, scalar2=None, op0=ALU.mult, deps=[dv])
        dv = E("vector", "memset", SV("state"), 0.0, deps=[dv])
        dv = E("vector", "memset", SV("onescol"), 1.0, deps=[dv])
        dw = E("scalar", "activation", out=SV("wdec"), in_=SV("wdec"),
               func=AF.Exp, scale=1.0, deps=[dld])
        dw = E("scalar", "activation", out=SV("wdec"), in_=SV("wdec"),
               func=AF.Exp, scale=-1.0, deps=[dw])
        b_os, bd = psum.acq()
        d_os = E("tensor", "matmul", ps_t[b_os][:, 0:1], ones1[:, :],
                 SV("os0", 0, 1), start=True, stop=True,
                 deps=[dld] + ([bd] if bd else []))
        d_osb = E("vector", "tensor_copy", STAT(0), ps_t[b_os][:, 0:1],
                  deps=[d_os, dv])
        psum.rel(b_os, d_osb)
        dv = E("vector", "tensor_scalar", out=SV("ds0"), in0=SV("ds0"),
               scalar1=STAT(0), scalar2=None, op0=ALU.mult, deps=[d_osb])
        DCONST = dv
        DW = dw

        def prep_weight(Wap, dout, din, scr, sign=False):
            nfo, nfi = dout // 128, din // 128
            slab_o = ar.alloc("pw_slab", 2 * din)
            stg_pool = ar
            STG = A
            stg_o = stg_pool.alloc("pw_stg", 2 * din)
            stg_free = [P.nowd("gpsimd"), P.nowd("gpsimd")]
            slab_free = [None, None]
            spill = None
            start_bar = [DCONST, P.now("tensor"), P.now("vector"),
                         P.now("scalar"), P.now("gpsimd"),
                         P.nowd("gpsimd"), P.nowd("scalar")]
            for o in range(nfo):
                sb = o % 2
                dl = P.dma("sync" if sb == 0 else "scalar",
                           A(slab_o + sb * din, din),
                           Wap[o * 128:(o + 1) * 128, :],
                           deps=[slab_free[sb]] if slab_free[sb] else start_bar)
                evs = []
                for fg in range(0, nfi, 4):
                    gw = min(4, nfi - fg)
                    b, bdep = psum.acq()
                    mmd = None
                    for j in range(gw):
                        f = fg + j
                        ddd = ([dl, DCONST] + ([bdep] if bdep else [])) if j == 0 else []
                        mmd = E("tensor", "transpose", ps_t[b][:, j * 128:(j + 1) * 128],
                                A(slab_o + sb * din + f * 128, 128), SV("ident"),
                                deps=ddd)
                    sdeps = [TN(), mmd] + ([stg_free[sb]] if stg_free[sb] else [])
                    if sign:
                        ed = E("scalar", "activation",
                               out=STG(stg_o + sb * din + fg * 128, gw * 128),
                               in_=ps_t[b][:, :gw * 128], func=AF.Sign, scale=1.0,
                               deps=sdeps)
                    elif (fg // 4) % 2 == 0:
                        ed = E("vector", "tensor_copy",
                               STG(stg_o + sb * din + fg * 128, gw * 128),
                               ps_t[b][:, :gw * 128], deps=sdeps)
                    else:
                        ed = E("scalar", "copy",
                               out=STG(stg_o + sb * din + fg * 128, gw * 128),
                               in_=ps_t[b][:, :gw * 128], deps=sdeps)
                    psum.rel(b, ed)
                    evs.append(ed)
                spill = P.dma("gpsimd", scr[o, :, :],
                              STG(stg_o + sb * din, din), deps=evs)
                slab_free[sb] = P.now("tensor")
                stg_free[sb] = spill
            ar.freev("pw_slab")
            stg_pool.freev("pw_stg")
            return spill

        prep = {}
        if RUN_A:
            prep["A"] = [prep_weight(Wk0, HID, D, WkT_s),
                         prep_weight(Wr0, HID, D, WrT_s),
                         prep_weight(Wv0, D, HID, WvT_s)]

        def proj(scr, nfo, nki, slab_w, rhs_ap, rhs_dep, tts, evac, wdeps):
            so = ar.alloc("proj_slab", 2 * slab_w)
            sfree = [None, None]
            start_bar = bar()
            odeps = [None] * nfo
            for o in range(nfo):
                sb = o % 2
                dl = P.dma("sync" if sb == 0 else "scalar",
                           A(so + sb * slab_w, slab_w), scr[o, :, :],
                           deps=(([sfree[sb]] if sfree[sb] else start_bar)
                                 + list(wdeps)))
                for (t0, tw) in tts:
                    b, bdep = psum.acq()
                    mmd = None
                    for f in range(nki):
                        ddd = ([dl, rhs_dep(f)] + ([bdep] if bdep else [])) if f == 0 \
                            else [rhs_dep(f)]
                        mmd = E("tensor", "matmul", ps_t[b][:, :tw],
                                A(so + sb * slab_w + f * 128, 128),
                                rhs_ap(f, t0, tw), start=(f == 0), stop=(f == nki - 1),
                                deps=ddd)
                    ed = evac(o, t0, tw, ps_t[b], TN())
                    psum.rel(b, ed)
                    odeps[o] = ed
                sfree[sb] = P.now("tensor")
            ar.freev("proj_slab")
            return odeps

        def ln_fm(chunks, w, writer, gamma=None, beta=None):
            bA, dA = psum.acq()
            bB, dB = psum.acq()
            sq_o = ar.alloc("ln_sq", NF * w)
            mm_a = mm_b = None
            sqds = []
            for f in range(NF):
                ap, dep = chunks[f]
                mm_a = E("tensor", "matmul", ps_t[bA][0:1, :w], SV("onescol"), ap,
                         start=(f == 0), stop=(f == NF - 1),
                         deps=[dep, DCONST] + ([dA] if (f == 0 and dA) else []))
            sq_bar = [P.now("tensor"), P.nowd("gpsimd"), P.now("scalar"),
                      P.nowd("scalar")]
            for f in range(NF):
                ap, dep = chunks[f]
                sqds.append(E("vector", "tensor_tensor", out=A(sq_o + f * w, w),
                              in0=ap, in1=ap, op=ALU.mult,
                              deps=[dep] + (sq_bar if f == 0 else [])))
            for f in range(NF):
                mm_b = E("tensor", "matmul", ps_t[bB][0:1, :w], SV("onescol"),
                         A(sq_o + f * w, w), start=(f == 0), stop=(f == NF - 1),
                         deps=[sqds[f]] + ([dB] if (f == 0 and dB) else []))
            prev_sq = mm_b
            ar.freev("ln_sq")
            st_o = ar.alloc("ln_st", 3 * w)
            e1 = A(st_o, w, 0, 1)
            e2 = A(st_o + w, w, 0, 1)
            e3 = A(st_o + 2 * w, w, 0, 1)
            d1 = E("vector", "tensor_scalar", out=e1, in0=ps_t[bA][0:1, :w],
                   scalar1=1.0 / D, scalar2=None, op0=ALU.mult,
                   deps=[TN(), mm_a, prev_sq])
            d2 = E("vector", "tensor_scalar", out=e2, in0=ps_t[bB][0:1, :w],
                   scalar1=1.0 / D, scalar2=EPS_LN, op0=ALU.mult, op1=ALU.add,
                   deps=[mm_b, d1])
            d3 = E("vector", "tensor_tensor", out=e3, in0=e1, in1=e1, op=ALU.mult,
                   deps=[d1, d2])
            d4 = E("vector", "tensor_tensor", out=e2, in0=e2, in1=e3,
                   op=ALU.subtract, deps=[d3])
            d5 = E("vector", "reciprocal", out=e3, in_=e2, deps=[d4])
            d6 = E("scalar", "activation", out=e2, in_=e3, func=AF.Sqrt, scale=1.0,
                   deps=[d5])
            d7 = E("vector", "tensor_tensor", out=e1, in0=e1, in1=e2, op=ALU.mult,
                   deps=[d6])
            psum.rel(bA, d1)
            psum.rel(bB, d2)
            bC, dC = psum.acq()
            bDD, dD = psum.acq()
            dbc = E("tensor", "matmul", ps_t[bC][:, :w], ones1[:, :], e2,
                    start=True, stop=True, deps=[d6, DCONST] + ([dC] if dC else []))
            dbd = E("tensor", "matmul", ps_t[bDD][:, :w], ones1[:, :], e1,
                    start=True, stop=True, deps=[d7] + ([dD] if dD else []))
            dbd = TN()
            ap_o = ar.alloc("ln_ap", w)
            outs = []
            prev_wr = None
            for f in range(NF):
                ap, dep = chunks[f]
                t1 = E("vector", "tensor_tensor", out=A(ap_o, w), in0=ap,
                       in1=ps_t[bC][:, :w], op=ALU.mult,
                       deps=[dbd, dep] + ([prev_wr] if prev_wr else []))
                t2 = E("vector", "tensor_tensor", out=A(ap_o, w), in0=A(ap_o, w),
                       in1=ps_t[bDD][:, :w], op=ALU.subtract, deps=[dbd, t1])
                if gamma is not None:
                    t2 = E("vector", "tensor_scalar", out=A(ap_o, w), in0=A(ap_o, w),
                           scalar1=gamma[0:128, f:f + 1], scalar2=beta[0:128, f:f + 1],
                           op0=ALU.mult, op1=ALU.add, deps=[t2])
                wd = writer(f, A(ap_o, w), t2)
                prev_wr = wd
                outs.append(wd)
            ar.freev("ln_st")
            ar.freev("ln_ap")
            psum.rel(bC, prev_wr)
            psum.rel(bDD, prev_wr)
            return outs

        def build_mix(dst_o, W, ss, tm_name, tm1_name, start_deps=()):
            xb_o = ar.alloc("mix_xb", 1024)
            last = None
            deps = [None] * NF
            first = True
            for f in range(NF):
                for t0 in range(0, W, 448):
                    tw = min(448, W - t0)
                    sd = list(start_deps) if first else []
                    dx = P.dma("sync", A(xb_o, tw),
                               xT[f * 128:(f + 1) * 128, ss + t0: ss + t0 + tw],
                               deps=([last] if last else []) + sd)
                    dxp = P.dma("sync", A(xb_o + 512, tw),
                                xpT[f * 128:(f + 1) * 128, ss + t0: ss + t0 + tw],
                                deps=[last] if last else [])
                    t1 = E("vector", "tensor_scalar", out=A(dst_o + f * W + t0, tw),
                           in0=A(xb_o + 512, tw), scalar1=SV(tm_name)[0:128, f:f + 1],
                           scalar2=None, op0=ALU.mult,
                           deps=[dxp, DCONST] + (sd if first else []))
                    t2 = E("vector", "scalar_tensor_tensor",
                           out=A(dst_o + f * W + t0, tw), in0=A(xb_o, tw),
                           scalar=SV(tm1_name)[0:128, f:f + 1],
                           in1=A(dst_o + f * W + t0, tw),
                           op0=ALU.mult, op1=ALU.add, deps=[dx, t1])
                    last = t2
                    deps[f] = t2
                    first = False
            P.wait("sync", last)
            ar.freev("mix_xb")
            return deps

        depA_done = []
        if RUN_A:
            state_dep = [None] * NH
            halves = [(0, 896, 256, 896), (896, 1792, 896, 1792)]
            for (ss, se, os_, oe) in halves:
                WK = se - ss
                WO = oe - os_
                mixk_o = ar.alloc("mixk", NF * WK)
                mdep = build_mix(mixk_o, WK, ss, "tmk", "tmk1", start_deps=bar())
                k_o = ar.alloc("k", NH * WK)
                tts = [(t0, min(448, WK - t0)) for t0 in range(0, WK, 448)]

                def ev_k(o, t0, tw, ps, mmd, k_o=k_o, WK=WK):
                    return E("scalar", "activation", out=A(k_o + o * WK + t0, tw),
                             in_=ps[:, :tw], func=AF.Relu, scale=1.0, deps=[mmd])
                kdeps = proj(WkT_s, NH, NF, D,
                             (lambda f, t0, tw, mixk_o=mixk_o, WK=WK:
                              A(mixk_o + f * WK + t0, tw)),
                             (lambda f, mdep=mdep: mdep[f]), tts, ev_k, prep["A"])
                ar.freev("mixk")
                for o in range(NH):
                    sd = E("vector", "tensor_tensor_scan", out=A(k_o + o * WK, WK),
                           data0=SV("wdec")[0:128, o:o + 1].broadcast_to((128, WK)),
                           data1=A(k_o + o * WK, WK),
                           initial=SV("state")[0:128, o:o + 1],
                           op0=ALU.mult, op1=ALU.add,
                           deps=[kdeps[o], DW, DCONST] +
                           ([state_dep[o]] if state_dep[o] else []))
                    sd = E("vector", "tensor_copy", SV("state")[0:128, o:o + 1],
                           A(k_o + o * WK + WK - 1, 1), deps=[sd])
                    state_dep[o] = sd
                    kdeps[o] = sd
                mixr_o = ar.alloc("mixr", NF * WO)
                rdep = build_mix(mixr_o, WO, os_, "tmr", "tmr1", start_deps=bar())
                koff = os_ - ss
                tts_o = [(t0, min(448, WO - t0)) for t0 in range(0, WO, 448)]

                def ev_r(o, t0, tw, ps, mmd, k_o=k_o, WK=WK, koff=koff,
                         kdeps=kdeps):
                    return E("vector", "scalar_tensor_tensor",
                             out=A(k_o + o * WK + koff + t0, tw), in0=ps[:, :tw],
                             scalar=1.0, in1=A(k_o + o * WK + koff + t0, tw),
                             op0=ALU.mult, op1=ALU.mult, deps=[mmd, kdeps[o]])
                rdeps = proj(WrT_s, NH, NF, D,
                             (lambda f, t0, tw, mixr_o=mixr_o, WO=WO:
                              A(mixr_o + f * WO + t0, tw)),
                             (lambda f, rdep=rdep: rdep[f]), tts_o, ev_r, prep["A"])
                ar.freev("mixr")
                vout_o = ar.alloc("vout", NF * WO)

                def ev_v(fo, t0, tw, ps, mmd, vout_o=vout_o, WO=WO):
                    return E("vector", "tensor_copy", A(vout_o + fo * WO + t0, tw),
                             ps[:, :tw], deps=[mmd])
                vdeps = proj(WvT_s, NF, NH, HID,
                             (lambda o, t0, tw, k_o=k_o, WK=WK, koff=koff:
                              A(k_o + o * WK + koff + t0, tw)),
                             (lambda o, rdeps=rdeps: rdeps[o]), tts_o, ev_v,
                             prep["A"])
                ar.freev("k")
                for (t0, tw) in tts_o:
                    chunks = [(A(vout_o + f * WO + t0, tw), vdeps[f]) for f in range(NF)]

                    def wr_back(f, ap, dep, vout_o=vout_o, WO=WO, t0=t0, tw=tw):
                        return E("vector", "tensor_copy",
                                 A(vout_o + f * WO + t0, tw), ap, deps=[dep])
                    o1 = ln_fm(chunks, tw, wr_back, gamma=SV("g0"), beta=SV("b0"))
                    xb2_o = ar.alloc("xb2", tw)
                    ch2 = []
                    lastr = None
                    for f in range(NF):
                        dx = P.dma("scalar", A(xb2_o, tw),
                                   xT[f * 128:(f + 1) * 128, os_ + t0: os_ + t0 + tw],
                                   deps=[o1[f]] + ([lastr] if lastr else []))
                        ad = E("vector", "tensor_tensor",
                               out=A(vout_o + f * WO + t0, tw),
                               in0=A(vout_o + f * WO + t0, tw),
                               in1=A(xb2_o, tw), op=ALU.add, deps=[dx, o1[f]])
                        lastr = ad
                        ch2.append((A(vout_o + f * WO + t0, tw), ad))
                    ar.freev("xb2")
                    col0 = (os_ - WARM) + t0

                    def wr_out(f, ap, dep, col0=col0, tw=tw):
                        return P.dma("gpsimd",
                                     x1_s[f * 128:(f + 1) * 128, col0:col0 + tw],
                                     ap, deps=[dep])
                    o2 = ln_fm(ch2, tw, wr_out)
                    depA_done.extend(o2)
                ar.freev("vout")

        depB_done = list(depA_done)
        if RUN_B:
            prep["B"] = [prep_weight(Bup, FF, D, BupT_s, sign=True),
                         prep_weight(Bdn, D, FF, BdnT_s, sign=True)]
            for sw in range(3):
                c0 = sw * 512
                W = 512
                x1_o = ar.alloc("bx1", NF * W)
                x1d = []
                lastl = None
                for f in range(NF):
                    dd = ([lastl] if lastl else []) + (bar() if f == 0 else [])
                    if sw == 0 and f == 0:
                        dd = dd + depA_done
                    dx = P.dma("sync", A(x1_o + f * W, W),
                               x1_s[f * 128:(f + 1) * 128, c0:c0 + W], deps=dd)
                    lastl = dx
                    x1d.append(dx)
                acc_o = ar.alloc("bacc", NF * W)
                accd = [None] * NF
                h_o = ar.alloc("bh", 8 * W)
                up_so = ar.alloc("bup_slab", 2 * D)
                dn_so = ar.alloc("bdn_slab", 2 * 1024)
                upfree = [None, None]
                dnfree = [None, None]
                for slab in range(8):
                    hdeps = [None] * 8
                    for ho in range(8):
                        o = slab * 8 + ho
                        sb = ho % 2
                        dl = P.dma("sync" if sb == 0 else "scalar",
                                   A(up_so + sb * D, D), BupT_s[o, :, :],
                                   deps=([upfree[sb]] if upfree[sb] else []) + prep["B"])
                        b, bdep = psum.acq()
                        mmd = None
                        for f in range(NF):
                            ddd = ([dl, x1d[f]] + ([bdep] if bdep else [])) if f == 0 \
                                else [x1d[f]]
                            mmd = E("tensor", "matmul", ps_t[b][:, :W],
                                    A(up_so + sb * D + f * 128, 128),
                                    A(x1_o + f * W, W), start=(f == 0),
                                    stop=(f == NF - 1), deps=ddd)
                        ed = E("scalar", "activation", out=A(h_o + ho * W, W),
                               in_=ps_t[b][:, :W], func=AF.Relu,
                               scale=SV("us0")[0:128, o:o + 1],
                               deps=[mmd, DCONST])
                        psum.rel(b, ed)
                        hdeps[ho] = ed
                        upfree[sb] = P.now("tensor")
                    for fo in range(NF):
                        sb = fo % 2
                        dl = P.dma("sync" if sb == 0 else "scalar",
                                   A(dn_so + sb * 1024, 1024),
                                   BdnT_s[fo, :, slab * 1024:(slab + 1) * 1024],
                                   deps=([dnfree[sb]] if dnfree[sb] else []) + prep["B"])
                        b, bdep = psum.acq()
                        mmd = None
                        for ho in range(8):
                            ddd = ([dl, hdeps[ho]] + ([bdep] if bdep else [])) if ho == 0 \
                                else [hdeps[ho]]
                            mmd = E("tensor", "matmul", ps_t[b][:, :W],
                                    A(dn_so + sb * 1024 + ho * 128, 128),
                                    A(h_o + ho * W, W), start=(ho == 0),
                                    stop=(ho == 7), deps=ddd)
                        if slab == 0:
                            ed = E("vector", "tensor_copy", A(acc_o + fo * W, W),
                                   ps_t[b][:, :W], deps=[mmd])
                        else:
                            ed = E("vector", "tensor_tensor", out=A(acc_o + fo * W, W),
                                   in0=A(acc_o + fo * W, W), in1=ps_t[b][:, :W],
                                   op=ALU.add, deps=[mmd, accd[fo]])
                        psum.rel(b, ed)
                        accd[fo] = ed
                        dnfree[sb] = P.now("tensor")
                ar.freev("bh")
                ar.freev("bup_slab")
                ar.freev("bdn_slab")
                ch2 = []
                for f in range(NF):
                    sd = E("vector", "tensor_scalar", out=A(acc_o + f * W, W),
                           in0=A(acc_o + f * W, W), scalar1=SV("ds0")[0:128, f:f + 1],
                           scalar2=None, op0=ALU.mult, deps=[accd[f], DCONST])
                    ad = E("vector", "tensor_tensor", out=A(acc_o + f * W, W),
                           in0=A(acc_o + f * W, W), in1=A(x1_o + f * W, W),
                           op=ALU.add, deps=[sd, x1d[f]])
                    ch2.append((A(acc_o + f * W, W), ad))

                def wr_x2(f, ap, dep, c0=c0, W=W):
                    return P.dma("gpsimd", x2_s[f * 128:(f + 1) * 128, c0:c0 + W],
                                 ap, deps=[dep])
                o2 = ln_fm(ch2, W, wr_x2)
                depB_done.extend(o2)
                ar.freev("bx1")
                ar.freev("bacc")

        depC_done = list(depB_done)
        if RUN_C:
            prep["C"] = [prep_weight(Wq1, D, D, WqT_s),
                         prep_weight(Wk1, KVH * DH, D, WkT1_s),
                         prep_weight(Wv1, KVH * DH, D, WvT1_s),
                         prep_weight(Wo1, D, D, WoT_s)]
            wC = prep["C"]
            kT_o = ar.alloc("kT", KVH * T_EXT)
            vT_o = ar.alloc("vT", KVH * T_EXT)
            x2b_o = ar.alloc("x2b", NF * 512)
            kslab_o = ar.alloc("kv_slab", 2 * D)
            ksfree = [None, None]
            kdeps1 = [None] * KVH
            vdeps1 = [None] * KVH
            for (t0, tw) in [(0, 512), (512, 512), (1024, 512)]:
                x2d = []
                lastl = None
                for f in range(NF):
                    dd = ([lastl] if lastl else []) + (bar() if f == 0 else [])
                    if t0 == 0 and f == 0:
                        dd = dd + depB_done
                    dx = P.dma("sync", A(x2b_o + f * 512, tw),
                               x2_s[f * 128:(f + 1) * 128, t0:t0 + tw], deps=dd)
                    lastl = dx
                    x2d.append(dx)
                for scr, dst_o, dst_deps in [(WkT1_s, kT_o, kdeps1),
                                             (WvT1_s, vT_o, vdeps1)]:
                    for g in range(KVH):
                        sb = g % 2
                        dl = P.dma("sync" if sb == 0 else "scalar",
                                   A(kslab_o + sb * D, D), scr[g, :, :],
                                   deps=([ksfree[sb]] if ksfree[sb] else []) + wC)
                        b, bdep = psum.acq()
                        mmd = None
                        for f in range(NF):
                            ddd = ([dl, x2d[f]] + ([bdep] if bdep else [])) if f == 0 \
                                else [x2d[f]]
                            mmd = E("tensor", "matmul", ps_t[b][:, :tw],
                                    A(kslab_o + sb * D + f * 128, 128),
                                    A(x2b_o + f * 512, tw), start=(f == 0),
                                    stop=(f == NF - 1), deps=ddd)
                        ed = E("vector", "tensor_copy",
                               A(dst_o + g * T_EXT + t0, tw), ps_t[b][:, :tw],
                               deps=[TN()])
                        psum.rel(b, ed)
                        dst_deps[g] = ed
                        ksfree[sb] = P.now("tensor")
            ar.freev("x2b")
            ar.freev("kv_slab")

            def rms_rope_fm(base_o, W, w1_ap, cos_o, sin_o, fold, dep_in):
                sq_o = ar.alloc("rr_sq", 512)
                st_o = ar.alloc("rr_st", 1024)
                rb_o = ar.alloc("rr_rb", 512)
                last = dep_in
                for t0 in range(0, W, 512):
                    tw = min(512, W - t0)
                    ap = A(base_o + t0, tw)
                    bS, dS = psum.acq()
                    sqd = E("vector", "tensor_tensor", out=A(sq_o, tw), in0=ap,
                            in1=ap, op=ALU.mult, deps=last)
                    mms = E("tensor", "matmul", ps_t[bS][0:1, :tw], SV("onescol"),
                            A(sq_o, tw), start=True, stop=True,
                            deps=[sqd, DCONST] + ([dS] if dS else []))
                    s1 = A(st_o, tw, 0, 1)
                    s2 = A(st_o + 512, tw, 0, 1)
                    d1 = E("vector", "tensor_scalar", out=s1, in0=ps_t[bS][0:1, :tw],
                           scalar1=1.0 / DH, scalar2=EPS_RMS, op0=ALU.mult,
                           op1=ALU.add, deps=[TN(), mms])
                    psum.rel(bS, d1)
                    d2 = E("vector", "reciprocal", out=s2, in_=s1, deps=[d1])
                    d3 = E("scalar", "activation", out=s1, in_=s2, func=AF.Sqrt,
                           scale=fold, deps=[d2])
                    bB, dB2 = psum.acq()
                    dbc = E("tensor", "matmul", ps_t[bB][:, :tw], ones1[:, :], s1,
                            start=True, stop=True, deps=[d3] + ([dB2] if dB2 else []))
                    n1 = E("vector", "tensor_tensor", out=ap, in0=ap,
                           in1=ps_t[bB][:, :tw], op=ALU.mult, deps=[TN(), dbc])
                    psum.rel(bB, n1)
                    n2 = E("vector", "tensor_scalar", out=ap, in0=ap, scalar1=w1_ap,
                           scalar2=None, op0=ALU.mult, deps=[n1, DCONST])
                    bR, dR = psum.acq()
                    mr = E("tensor", "matmul", ps_t[bR][:, :tw], SV("rot"), ap,
                           start=True, stop=True,
                           deps=[n2, DCONST] + ([dR] if dR else []))
                    mr = TN()
                    r1 = E("vector", "tensor_tensor", out=A(rb_o, tw), in0=ap,
                           in1=A(cos_o + t0, tw), op=ALU.mult, deps=[mr])
                    r2 = E("vector", "scalar_tensor_tensor", out=ap,
                           in0=ps_t[bR][:, :tw], scalar=1.0,
                           in1=A(sin_o + t0, tw), op0=ALU.mult, op1=ALU.mult,
                           deps=[r1])
                    r3 = E("vector", "tensor_tensor", out=ap, in0=ap,
                           in1=A(rb_o, tw), op=ALU.add, deps=[r2])
                    psum.rel(bR, r3)
                    last = [r3]
                ar.freev("rr_sq")
                ar.freev("rr_st")
                ar.freev("rr_rb")
                return last[0]

            rms_bar = bar()
            for g in range(KVH):
                kdeps1[g] = rms_rope_fm(
                    kT_o + g * T_EXT, T_EXT, SV("w1k")[0:128, 0:1],
                    p_cosk, p_sink, 1.0, [kdeps1[g]] + rms_bar)
            vtm_o = ar.alloc("vtm", NKT * 512)
            vtmd = [None] * NKT
            for kt in range(NKT):
                for g in range(KVH):
                    b, bdep = psum.acq()
                    md = E("tensor", "transpose", ps_t[b][:, 0:128],
                           A(vT_o + g * T_EXT + kt * 128, 128), SV("ident"),
                           deps=[vdeps1[g], DCONST] + ([bdep] if bdep else []))
                    ed = E("vector", "tensor_copy",
                           A(vtm_o + kt * 512 + g * 128, 128), ps_t[b][:, 0:128],
                           deps=[TN()])
                    psum.rel(b, ed)
                    vtmd[kt] = ed
            ar.freev("vT")

            x3deps = []
            laste0 = None
            lastctx = None
            for grp in range(NQT // 2):
                q0 = grp * 256
                qT_o = ar.alloc("qT", HEADS * 256)
                x2q_o = ar.alloc("x2q", NF * 256)
                qd = []
                lastl = None
                for f in range(NF):
                    dd = ([lastl] if lastl else []) + (bar() if f == 0 else [])
                    dx = P.dma("sync", A(x2q_o + f * 256, 256),
                               x2_s[f * 128:(f + 1) * 128, HALO + q0: HALO + q0 + 256],
                               deps=dd)
                    lastl = dx
                    qd.append(dx)

                def ev_q(j, t0, tw, ps, mmd, qT_o=qT_o):
                    return E("vector", "tensor_copy", A(qT_o + j * 256, tw),
                             ps[:, :tw], deps=[mmd])
                qdeps = proj(WqT_s, HEADS, NF, D,
                             (lambda f, t0, tw, x2q_o=x2q_o: A(x2q_o + f * 256, tw)),
                             (lambda f, qd=qd: qd[f]), [(0, 256)], ev_q, wC)
                for j in range(HEADS):
                    qdeps[j] = rms_rope_fm(
                        qT_o + j * 256, 256, SV("w1q")[0:128, 0:1],
                        p_cosq + q0, p_sinq + q0,
                        1.0 / float(DH), [qdeps[j]])
                ar.freev("x2q")
                ctx_o = ar.alloc("ctx", HEADS * 256)
                e0_o = ar.alloc("att_e0", KSPAN)
                at_o = ar.alloc("att_t", KSPAN)
                ctxdeps = [None] * HEADS
                last_exp = None
                for j in range(HEADS):
                    g = j // G
                    for qi in range(2):
                        qt = grp * 2 + qi
                        bA_, dA_ = psum.acq()
                        bB_, dB_ = psum.acq()
                        m1 = E("tensor", "matmul", ps_t[bA_][:, :512],
                               A(qT_o + j * 256 + qi * 128, 128),
                               A(kT_o + g * T_EXT + qt * 128, 512),
                               start=True, stop=True,
                               deps=[qdeps[j], kdeps1[g]] + ([dA_] if dA_ else []))
                        m2 = E("tensor", "matmul", ps_t[bB_][:, :128],
                               A(qT_o + j * 256 + qi * 128, 128),
                               A(kT_o + g * T_EXT + qt * 128 + 512, 128),
                               start=True, stop=True, deps=[dB_] if dB_ else [])
                        m2f = TN()
                        a1 = E("vector", "tensor_tensor", out=ps_t[bA_][:, :512],
                               in0=ps_t[bA_][:, :512],
                               in1=A(p_mask + qt * KSPAN, 512), op=ALU.add,
                               deps=[m2f, m1, DCONST])
                        a2 = E("vector", "tensor_tensor", out=ps_t[bB_][:, :128],
                               in0=ps_t[bB_][:, :128],
                               in1=A(p_mask + qt * KSPAN + 512, 128), op=ALU.add,
                               deps=[m2, a1])
                        nm1, nm2, sa, sb_ = STAT(1), STAT(2), STAT(3), STAT(4)
                        r1 = E("vector", "tensor_reduce", nm1, ps_t[bA_][:, :512],
                               axis=AX.X, op=ALU.max, negate=True,
                               deps=[a1] + ([last_exp] if last_exp else []))
                        r2 = E("vector", "tensor_reduce", nm2, ps_t[bB_][:, :128],
                               axis=AX.X, op=ALU.max, negate=True, deps=[a2, r1])
                        r3 = E("vector", "tensor_tensor", out=nm1, in0=nm1, in1=nm2,
                               op=ALU.min, deps=[r2])
                        x1_ = E("scalar", "activation", out=A(e0_o, 512),
                                in_=ps_t[bA_][:, :512], func=AF.Exp, bias=nm1,
                                scale=1.0, accum_out=sa,
                                deps=[r3] + ([laste0] if laste0 else []))
                        x2_ = E("scalar", "activation", out=A(e0_o + 512, 128),
                                in_=ps_t[bB_][:, :128], func=AF.Exp, bias=nm1,
                                scale=1.0, accum_out=sb_, deps=[x1_])
                        last_exp = x2_
                        psum.rel(bA_, x1_)
                        psum.rel(bB_, x2_)
                        s1 = E("vector", "tensor_tensor", out=sa, in0=sa, in1=sb_,
                               op=ALU.add, deps=[x2_])
                        s2 = E("vector", "reciprocal", out=sb_, in_=sa, deps=[s1])
                        n1 = E("vector", "tensor_scalar", out=A(e0_o, KSPAN),
                               in0=A(e0_o, KSPAN), scalar1=sb_, scalar2=None,
                               op0=ALU.mult, deps=[s2])
                        tds = []
                        for kb in range(5):
                            bT_, dT_ = psum.acq()
                            td = E("tensor", "transpose", ps_t[bT_][:, 0:128],
                                   A(e0_o + kb * 128, 128), SV("ident"),
                                   deps=[n1, DCONST] + ([dT_] if dT_ else []))
                            te = E("vector", "tensor_copy", A(at_o + kb * 128, 128),
                                   ps_t[bT_][:, 0:128],
                                   deps=[TN()] + ([lastctx] if lastctx else []))
                            psum.rel(bT_, te)
                            tds.append(te)
                        laste0 = tds[-1]
                        bC_, dC_ = psum.acq()
                        ctxd = None
                        for kb in range(5):
                            kt = qt + kb
                            ddd = [tds[kb], vtmd[kt], DCONST]
                            if kb == 0 and dC_:
                                ddd.append(dC_)
                            ctxd = E("tensor", "matmul", ps_t[bC_][:, :128],
                                     A(vtm_o + kt * 512 + g * 128, 128),
                                     A(at_o + kb * 128, 128),
                                     start=(kb == 0), stop=(kb == 4), deps=ddd)
                        ce = E("vector", "tensor_copy",
                               A(ctx_o + j * 256 + qi * 128, 128),
                               ps_t[bC_][:, :128], deps=[TN()])
                        psum.rel(bC_, ce)
                        lastctx = ctxd
                        ctxdeps[j] = ce
                ar.freev("att_e0")
                ar.freev("att_t")
                ar.freev("qT")
                out_o = ar.alloc("cout", NF * 256)

                def ev_o(fo, t0, tw, ps, mmd, out_o=out_o):
                    return E("vector", "tensor_copy", A(out_o + fo * 256, tw),
                             ps[:, :tw], deps=[mmd])
                odeps = proj(WoT_s, NF, HEADS, D,
                             (lambda j, t0, tw, ctx_o=ctx_o: A(ctx_o + j * 256, tw)),
                             (lambda j, ctxdeps=ctxdeps: ctxdeps[j]),
                             [(0, 256)], ev_o, wC)
                ar.freev("ctx")
                xr_o = ar.alloc("cxr", 256)
                ch2 = []
                lastx = None
                for f in range(NF):
                    dx = P.dma("scalar", A(xr_o, 256),
                               x2_s[f * 128:(f + 1) * 128, HALO + q0: HALO + q0 + 256],
                               deps=[odeps[f]] + ([lastx] if lastx else []))
                    ad = E("vector", "tensor_tensor", out=A(out_o + f * 256, 256),
                           in0=A(out_o + f * 256, 256), in1=A(xr_o, 256),
                           op=ALU.add, deps=[dx, odeps[f]])
                    lastx = ad
                    ch2.append((A(out_o + f * 256, 256), ad))
                ar.freev("cxr")

                def wr_x3(f, ap, dep, q0=q0):
                    return P.dma("gpsimd", x3_s[f * 128:(f + 1) * 128, q0:q0 + 256],
                                 ap, deps=[dep])
                o3 = ln_fm(ch2, 256, wr_x3)
                x3deps.extend(o3)
                ar.freev("cout")
            depC_done = x3deps
            ar.freev("kT")
            ar.freev("vtm")

        if RUN_D:
            prep["D"] = [prep_weight(Uup, FF, D, UupT_s),
                         prep_weight(Ug, FF, D, UgT_s),
                         prep_weight(Udn, D, FF, UdnT_s)]
            wD = prep["D"]
            TC = 256
            for sw in range(T_OWN // TC):
                c0 = sw * TC
                x3_o = ar.alloc("dx3", NF * TC)
                xs_o = ar.alloc("dxs", NF * TC)
                x3d = []
                lastl = None
                for f in range(NF):
                    dd = ([lastl] if lastl else []) + (bar() if f == 0 else [])
                    if sw == 0 and f == 0:
                        dd = dd + depC_done
                    dx = P.dma("sync", A(x3_o + f * TC, TC),
                               x3_s[f * 128:(f + 1) * 128, c0:c0 + TC], deps=dd)
                    lastl = dx
                    x3d.append(dx)
                xsd = []
                for f in range(NF):
                    sd = E("vector", "tensor_scalar", out=A(xs_o + f * TC, TC),
                           in0=A(x3_o + f * TC, TC),
                           scalar1=SV("sug")[0:128, f:f + 1], scalar2=None,
                           op0=ALU.mult, deps=[x3d[f], DCONST])
                    xsd.append(sd)

                def fwht(base_o, n_chunks, deps_list, TC=TC):
                    nst = {16: 4, 64: 6}[n_chunks]
                    for st in range(nst):
                        s = 1 << st
                        ndep = list(deps_list)
                        for i in range(n_chunks):
                            if i & s:
                                continue
                            a_ap = A(base_o + i * TC, TC)
                            b_ap = A(base_o + (i | s) * TC, TC)
                            d1 = E("vector", "tensor_tensor", out=b_ap, in0=a_ap,
                                   in1=b_ap, op=ALU.subtract,
                                   deps=[deps_list[i], deps_list[i | s]])
                            d2 = E("vector", "scalar_tensor_tensor", out=a_ap,
                                   in0=a_ap, scalar=2.0, in1=b_ap,
                                   op0=ALU.mult, op1=ALU.subtract, deps=[d1])
                            ndep[i] = d2
                            ndep[i | s] = d2
                        deps_list = ndep
                    out = []
                    for i in range(n_chunks):
                        b, bdep = psum.acq()
                        md = E("tensor", "matmul", ps_t[b][:, :TC], SV("h128"),
                               A(base_o + i * TC, TC), start=True, stop=True,
                               deps=[deps_list[i], DCONST] + ([bdep] if bdep else []))
                        ed = E("vector", "tensor_copy", A(base_o + i * TC, TC),
                               ps_t[b][:, :TC], deps=[TN(), md])
                        psum.rel(b, ed)
                        out.append(ed)
                    return out

                xsd = fwht(xs_o, NF, xsd)
                z_o = ar.alloc("dz", NQ * TC)
                us_o = ar.alloc("dup_slab", 2 * D)
                gs_o = ar.alloc("dg_slab", 2 * D)
                usfree = [None, None]
                gsfree = [None, None]
                zdeps = [None] * NQ
                prev_sg = None
                for o in range(NQ):
                    sb = o % 2
                    qe = "sync" if sb == 0 else "scalar"
                    dlu = P.dma(qe, A(us_o + sb * D, D), UupT_s[o, :, :],
                                deps=([usfree[sb]] if usfree[sb] else []) + wD)
                    dlg = P.dma(qe, A(gs_o + sb * D, D), UgT_s[o, :, :],
                                deps=([gsfree[sb]] if gsfree[sb] else []) + wD)
                    bu, bud = psum.acq()
                    bg, bgd = psum.acq()
                    mmu = mmg = None
                    for f in range(NF):
                        du = ([dlu, xsd[f]] + ([bud] if bud else [])) if f == 0 else [xsd[f]]
                        mmu = E("tensor", "matmul", ps_t[bu][:, :TC],
                                A(us_o + sb * D + f * 128, 128), A(xs_o + f * TC, TC),
                                start=(f == 0), stop=(f == NF - 1), deps=du)
                        dg = ([dlg] + ([bgd] if bgd else [])) if f == 0 else []
                        mmg = E("tensor", "matmul", ps_t[bg][:, :TC],
                                A(gs_o + sb * D + f * 128, 128), A(xs_o + f * TC, TC),
                                start=(f == 0), stop=(f == NF - 1), deps=dg)
                    mmf = TN()
                    sgd = E("scalar", "activation", out=SV("dsg"),
                            in_=ps_t[bg][:, :TC], func=AF.Silu,
                            scale=SV("gsg")[0:128, o:o + 1],
                            deps=[mmf, mmg, DCONST] + ([prev_sg] if prev_sg else []))
                    zd = E("vector", "scalar_tensor_tensor", out=A(z_o + o * TC, TC),
                           in0=ps_t[bu][:, :TC], scalar=SV("uscomb")[0:128, o:o + 1],
                           in1=SV("dsg"), op0=ALU.mult, op1=ALU.mult,
                           deps=[mmf, sgd])
                    prev_sg = zd
                    psum.rel(bu, zd)
                    psum.rel(bg, sgd)
                    zdeps[o] = zd
                    usfree[sb] = P.now("tensor")
                    gsfree[sb] = P.now("tensor")
                ar.freev("dup_slab")
                ar.freev("dg_slab")
                ar.freev("dxs")
                zdeps = fwht(z_o, NQ, zdeps)
                dn_o = ar.alloc("ddn_slab", 2 * 1024)
                dn_bar = bar()
                dnfree = [None, None]
                out_o = ar.alloc("dout", NF * TC)
                outd = [None] * NF
                for fo in range(NF):
                    b, bdep = psum.acq()
                    accmm = None
                    for og in range(8):
                        sb = og % 2
                        dl = P.dma("sync" if sb == 0 else "scalar",
                                   A(dn_o + sb * 1024, 1024),
                                   UdnT_s[fo, :, og * 1024:(og + 1) * 1024],
                                   deps=([dnfree[sb]] if dnfree[sb] else dn_bar) + wD)
                        for hh in range(8):
                            o = og * 8 + hh
                            ddd = [dl, zdeps[o]]
                            if o == 0 and bdep:
                                ddd.append(bdep)
                            accmm = E("tensor", "matmul", ps_t[b][:, :TC],
                                      A(dn_o + sb * 1024 + hh * 128, 128),
                                      A(z_o + o * TC, TC), start=(o == 0),
                                      stop=(o == NQ - 1), deps=ddd)
                        dnfree[sb] = P.now("tensor")
                    sd = E("vector", "tensor_scalar", out=A(out_o + fo * TC, TC),
                           in0=ps_t[b][:, :TC], scalar1=SV("wsd")[0:128, fo:fo + 1],
                           scalar2=None, op0=ALU.mult, deps=[TN(), accmm, DCONST])
                    psum.rel(b, sd)
                    ad = E("vector", "tensor_tensor", out=A(out_o + fo * TC, TC),
                           in0=A(out_o + fo * TC, TC), in1=A(x3_o + fo * TC, TC),
                           op=ALU.add, deps=[sd, x3d[fo]])
                    outd[fo] = ad
                ar.freev("ddn_slab")
                ar.freev("dz")
                ch = [(A(out_o + f * TC, TC), outd[f]) for f in range(NF)]

                def wr_x4(f, ap, dep, c0=c0, TC=TC):
                    return P.dma("gpsimd", x4T[f * 128:(f + 1) * 128, c0:c0 + TC],
                                 ap, deps=[dep])
                ln_fm(ch, TC, wr_x4)
                ar.freev("dout")
                ar.freev("dx3")

        P.wait("gpsimd", P.nowd("gpsimd"))
        P.wait("gpsimd", P.nowd("sync"))
        P.wait("gpsimd", P.nowd("scalar"))
        P.wait("gpsimd", P.now("gpsimd"))
        P.wait("scalar", P.now("scalar"))

        with nc.Block() as block:
            def mk(eng_name):
                def body(e):
                    for item in P.ops[eng_name]:
                        if item[0] == "wait":
                            e.wait_ge(semd[item[1]] if item[2] else sems[item[1]],
                                      item[3])
                        else:
                            item[1](e).then_inc(
                                semd[eng_name] if item[2] == 16 else sems[eng_name],
                                item[2])
                return body
            block.tensor(mk("tensor"))
            block.vector(mk("vector"))
            block.scalar(mk("scalar"))
            block.gpsimd(mk("gpsimd"))
            block.sync(mk("sync"))
    return nc


def _consts():
    ident = np.eye(128, dtype=np.float32)
    h = np.array([[1.0]], dtype=np.float32)
    while h.shape[0] < 128:
        h = np.block([[h, h], [h, -h]])
    rotT = np.zeros((128, 128), dtype=np.float32)
    for p in range(64):
        rotT[p + 64, p] = -1.0
        rotT[p, p + 64] = 1.0
    ones1 = np.ones((1, 128), dtype=np.float32)
    return ident, np.ascontiguousarray(h, dtype=np.float32), rotT, ones1


def _rope_tables():
    inv = 1.0 / (10000.0 ** (np.arange(0, DH, 2, dtype=np.float64) / DH))
    pos = np.arange(TSEQ, dtype=np.float64)
    fr = pos[:, None] * inv[None, :]
    emb = np.concatenate([fr, fr], axis=-1)
    return np.cos(emb).astype(np.float32), np.sin(emb).astype(np.float32)


def _mask_for_core(c):
    m = np.zeros((NQT, 128, KSPAN), dtype=np.float32)
    NEG = np.float32(np.finfo(np.float32).min)
    jj = np.arange(KSPAN)
    for qt in range(NQT):
        for i in range(128):
            valid = (jj >= i) & (jj <= i + WINDOW)
            if c == 0:
                valid &= (qt * 128 + jj) >= HALO
            m[qt, i, :] = np.where(valid, 0.0, NEG)
    return m


def _vec_rs(v, nch):
    return np.ascontiguousarray(np.asarray(v, dtype=np.float32).reshape(nch, 128).T)


def kernel(**inputs):
    x = np.asarray(inputs["L0_attn_x"], dtype=np.float32)[0]
    xT_full = np.ascontiguousarray(x.T)
    xp = np.concatenate([x[1:2], x[:-1]], axis=0)
    xpT_full = np.ascontiguousarray(xp.T)
    ident, h128, rotT, ones1 = _consts()
    cos_t, sin_t = _rope_tables()

    common = dict(
        Wk0=np.ascontiguousarray(inputs["L0_attn_Wk"], dtype=np.float32),
        Wr0=np.ascontiguousarray(inputs["L0_attn_Wr"], dtype=np.float32),
        Wv0=np.ascontiguousarray(inputs["L0_attn_Wv"], dtype=np.float32),
        Bup=np.ascontiguousarray(inputs["L0_ffn_up_weight"], dtype=np.float32),
        Bdn=np.ascontiguousarray(inputs["L0_ffn_down_weight"], dtype=np.float32),
        Wq1=np.ascontiguousarray(inputs["L1_attn_Wq"], dtype=np.float32),
        Wk1=np.ascontiguousarray(inputs["L1_attn_Wk"], dtype=np.float32),
        Wv1=np.ascontiguousarray(inputs["L1_attn_Wv"], dtype=np.float32),
        Wo1=np.ascontiguousarray(inputs["L1_attn_Wo"], dtype=np.float32),
        Uup=np.ascontiguousarray(inputs["L1_ffn_W_up"], dtype=np.float32),
        Ug=np.ascontiguousarray(inputs["L1_ffn_W_gate"], dtype=np.float32),
        Udn=np.ascontiguousarray(inputs["L1_ffn_W_down"], dtype=np.float32),
        v_tmk=_vec_rs(inputs["L0_attn_time_mix_k"], NF),
        v_tmr=_vec_rs(inputs["L0_attn_time_mix_r"], NF),
        v_td=_vec_rs(inputs["L0_attn_time_decay"], NH),
        v_g0=_vec_rs(inputs["L0_attn_ln_gamma"], NF),
        v_b0=_vec_rs(inputs["L0_attn_ln_beta"], NF),
        v_us0=_vec_rs(inputs["L0_ffn_up_scale"], NQ),
        v_ds0=_vec_rs(inputs["L0_ffn_down_scale"], NF),
        v_os0=np.ascontiguousarray(np.asarray(inputs["L0_ffn_output_scale"],
                                              dtype=np.float32).reshape(1, 1)),
        v_sug=_vec_rs(inputs["L1_ffn_SU_ug"], NF),
        v_sdp=_vec_rs(inputs["L1_ffn_SU_dp"], NQ),
        v_wsu=_vec_rs(np.asarray(inputs["L1_ffn_Wscale_ug"])[:FF], NQ),
        v_wsg=_vec_rs(np.asarray(inputs["L1_ffn_Wscale_ug"])[FF:], NQ),
        v_wsd=_vec_rs(inputs["L1_ffn_Wscale_dp"], NF),
        v_w1q=np.ascontiguousarray(
            (1.0 + np.asarray(inputs["L1_attn_qn_w"], dtype=np.float32)).reshape(128, 1)),
        v_w1k=np.ascontiguousarray(
            (1.0 + np.asarray(inputs["L1_attn_kn_w"], dtype=np.float32)).reshape(128, 1)),
        c_ident=ident, c_h128=h128, c_rot=rotT, c_ones1=ones1,
    )

    in_maps = []
    for c in range(NCORES):
        t0 = c * T_OWN
        lo = t0 - (HALO + WARM)
        sl_x = np.zeros((D, T_SCAN), dtype=np.float32)
        sl_xp = np.zeros((D, T_SCAN), dtype=np.float32)
        s0 = max(0, lo)
        sl_x[:, s0 - lo:] = xT_full[:, s0:t0 + T_OWN]
        sl_xp[:, s0 - lo:] = xpT_full[:, s0:t0 + T_OWN]
        ke = t0 - HALO
        cosk = np.zeros((T_EXT, DH), dtype=np.float32)
        sink = np.zeros((T_EXT, DH), dtype=np.float32)
        k0 = max(0, ke)
        cosk[k0 - ke:, :] = cos_t[k0:t0 + T_OWN]
        sink[k0 - ke:, :] = sin_t[k0:t0 + T_OWN]
        m = dict(common)
        m.update(xT=sl_x, xpT=sl_xp,
                 c_cosqT=np.ascontiguousarray(cos_t[t0:t0 + T_OWN].T),
                 c_sinqT=np.ascontiguousarray(sin_t[t0:t0 + T_OWN].T),
                 c_coskT=np.ascontiguousarray(cosk.T),
                 c_sinkT=np.ascontiguousarray(sink.T),
                 c_mask=_mask_for_core(c))
        in_maps.append(m)

    nc = build_program()
    res = run_bass_kernel_spmd(nc, in_maps, core_ids=list(range(NCORES)))
    shards = [np.asarray(r["x4T"]) for r in res.results]
    out = np.concatenate(shards, axis=1).T.reshape(1, TSEQ, D)
    return np.ascontiguousarray(out).astype(np.float32)


# version 11
# speedup vs baseline: 1.1616x; 1.1616x over previous
import numpy as np
from contextlib import ExitStack
import concourse.bass as bass
import concourse.mybir as mybir
from concourse.bass_utils import run_bass_kernel_spmd

F32 = mybir.dt.float32
F32R = mybir.dt.float32r
ALU = mybir.AluOpType
AF = mybir.ActivationFunctionType
AX = mybir.AxisListType

D = 2048
HID = 2816
FF = 8192
TSEQ = 8192
NCORES = 8
T_OWN = TSEQ // NCORES
HALO = 512
WARM = 256
T_EXT = T_OWN + HALO
T_SCAN = T_EXT + WARM
NF = D // 128
NH = HID // 128
NQ = FF // 128
HEADS, KVH, DH = 16, 4, 128
G = HEADS // KVH
WINDOW = 512
KSPAN = WINDOW + 128
NQT = T_OWN // 128
NKT = T_EXT // 128
EPS_LN = 1e-5
EPS_RMS = 1e-6

RUN_A = True
RUN_B = True
RUN_C = True
RUN_D = True

ENGS = ("tensor", "vector", "scalar", "gpsimd", "sync")


class Dep:
    __slots__ = ("eng", "cnt", "dma")

    def __init__(self, eng, cnt, dma=False):
        self.eng, self.cnt, self.dma = eng, cnt, dma


class Prog:
    def __init__(self):
        self.ops = {e: [] for e in ENGS}
        self.cnt = {e: 0 for e in ENGS}
        self.cntd = {e: 0 for e in ENGS}
        self.hwm = {e: {} for e in ENGS}

    def wait(self, eng, dep):
        if dep is None:
            return
        tgt = self.cntd[dep.eng] if dep.dma else dep.cnt
        key = (dep.eng, dep.dma)
        if not dep.dma and dep.eng == eng:
            assert tgt <= self.cnt[eng], "same-engine forward wait"
        elif self.hwm[eng].get(key, 0) >= tgt and not (dep.eng == eng and not dep.dma):
            return
        if self.hwm[eng].get(key, 0) >= tgt and dep.eng != eng:
            return
        self.ops[eng].append(("wait", dep.eng, dep.dma, tgt))
        self.hwm[eng][key] = max(self.hwm[eng].get(key, 0), tgt)

    def emit(self, eng, fn, deps=(), inc=1):
        for d in deps:
            self.wait(eng, d)
        self.ops[eng].append(("op", fn, inc))
        if inc == 16:
            self.cntd[eng] += 16
        else:
            self.cnt[eng] += inc
        return Dep(eng, self.cnt[eng])

    def dma(self, eng, out_ap, in_ap, deps=()):
        self.emit(eng, (lambda e, o=out_ap, i=in_ap: e.dma_start(out=o, in_=i)),
                  deps=deps, inc=16)
        return Dep(eng, self.cntd[eng], dma=True)

    def now(self, eng):
        return Dep(eng, self.cnt[eng])

    def nowd(self, eng):
        return Dep(eng, self.cntd[eng], dma=True)


class Arena:
    def __init__(self, width):
        self.width = width
        self.free = [(0, width)]
        self.live = {}

    def alloc(self, name, w, align=8):
        assert name not in self.live
        for i, (o, fw) in enumerate(self.free):
            oo = (o + align - 1) // align * align
            if oo + w <= o + fw:
                repl = []
                if oo > o:
                    repl.append((o, oo - o))
                if o + fw > oo + w:
                    repl.append((oo + w, o + fw - (oo + w)))
                self.free[i:i + 1] = repl
                self.live[name] = (oo, w)
                return oo
        raise MemoryError(f"arena OOM {name} w={w} live={sorted(self.live.items())}")

    def freev(self, name):
        o, w = self.live.pop(name)
        self.free.append((o, w))
        self.free.sort()
        m = []
        for a, b in self.free:
            if m and m[-1][0] + m[-1][1] == a:
                m[-1] = (m[-1][0], m[-1][1] + b)
            else:
                m.append((a, b))
        self.free = m


def build_program():
    nc = bass.Bass("TRN2", target_bir_lowering=False)
    P = Prog()
    dt = nc.dram_tensor

    xT = dt("xT", [D, T_SCAN], F32, kind="ExternalInput").ap()
    xpT = dt("xpT", [D, T_SCAN], F32, kind="ExternalInput").ap()
    Wk0 = dt("Wk0", [HID, D], F32, kind="ExternalInput").ap()
    Wr0 = dt("Wr0", [HID, D], F32, kind="ExternalInput").ap()
    Wv0 = dt("Wv0", [D, HID], F32, kind="ExternalInput").ap()
    Bup = dt("Bup", [FF, D], F32, kind="ExternalInput").ap()
    Bdn = dt("Bdn", [D, FF], F32, kind="ExternalInput").ap()
    Wq1 = dt("Wq1", [D, D], F32, kind="ExternalInput").ap()
    Wk1 = dt("Wk1", [KVH * DH, D], F32, kind="ExternalInput").ap()
    Wv1 = dt("Wv1", [KVH * DH, D], F32, kind="ExternalInput").ap()
    Wo1 = dt("Wo1", [D, D], F32, kind="ExternalInput").ap()
    Uup = dt("Uup", [FF, D], F32, kind="ExternalInput").ap()
    Ug = dt("Ug", [FF, D], F32, kind="ExternalInput").ap()
    Udn = dt("Udn", [D, FF], F32, kind="ExternalInput").ap()
    v_tmk = dt("v_tmk", [128, NF], F32, kind="ExternalInput").ap()
    v_tmr = dt("v_tmr", [128, NF], F32, kind="ExternalInput").ap()
    v_td = dt("v_td", [128, NH], F32, kind="ExternalInput").ap()
    v_g0 = dt("v_g0", [128, NF], F32, kind="ExternalInput").ap()
    v_b0 = dt("v_b0", [128, NF], F32, kind="ExternalInput").ap()
    v_us0 = dt("v_us0", [128, NQ], F32, kind="ExternalInput").ap()
    v_ds0 = dt("v_ds0", [128, NF], F32, kind="ExternalInput").ap()
    v_os0 = dt("v_os0", [1, 1], F32, kind="ExternalInput").ap()
    v_sug = dt("v_sug", [128, NF], F32, kind="ExternalInput").ap()
    v_sdp = dt("v_sdp", [128, NQ], F32, kind="ExternalInput").ap()
    v_wsu = dt("v_wsu", [128, NQ], F32, kind="ExternalInput").ap()
    v_wsg = dt("v_wsg", [128, NQ], F32, kind="ExternalInput").ap()
    v_wsd = dt("v_wsd", [128, NF], F32, kind="ExternalInput").ap()
    v_w1q = dt("v_w1q", [128, 1], F32, kind="ExternalInput").ap()
    v_w1k = dt("v_w1k", [128, 1], F32, kind="ExternalInput").ap()
    c_ident = dt("c_ident", [128, 128], F32, kind="ExternalInput").ap()
    c_h128 = dt("c_h128", [128, 128], F32, kind="ExternalInput").ap()
    c_rot = dt("c_rot", [128, 128], F32, kind="ExternalInput").ap()
    c_ones1 = dt("c_ones1", [1, 128], F32, kind="ExternalInput").ap()
    c_cosqT = dt("c_cosqT", [DH, T_OWN], F32, kind="ExternalInput").ap()
    c_sinqT = dt("c_sinqT", [DH, T_OWN], F32, kind="ExternalInput").ap()
    c_coskT = dt("c_coskT", [DH, T_EXT], F32, kind="ExternalInput").ap()
    c_sinkT = dt("c_sinkT", [DH, T_EXT], F32, kind="ExternalInput").ap()
    c_mask = dt("c_mask", [NQT, 128, KSPAN], F32, kind="ExternalInput").ap()

    WkT_s = dt("WkT_s", [NH, 128, D], F32, kind="Internal").ap()
    WrT_s = dt("WrT_s", [NH, 128, D], F32, kind="Internal").ap()
    WvT_s = dt("WvT_s", [NF, 128, HID], F32, kind="Internal").ap()
    BupT_s = dt("BupT_s", [NQ, 128, D], F32, kind="Internal").ap()
    BdnT_s = dt("BdnT_s", [NF, 128, FF], F32, kind="Internal").ap()
    WqT_s = dt("WqT_s", [NF, 128, D], F32, kind="Internal").ap()
    WkT1_s = dt("WkT1_s", [KVH, 128, D], F32, kind="Internal").ap()
    WvT1_s = dt("WvT1_s", [KVH, 128, D], F32, kind="Internal").ap()
    WoT_s = dt("WoT_s", [NF, 128, D], F32, kind="Internal").ap()
    UupT_s = dt("UupT_s", [NQ, 128, D], F32, kind="Internal").ap()
    UgT_s = dt("UgT_s", [NQ, 128, D], F32, kind="Internal").ap()
    UdnT_s = dt("UdnT_s", [NF, 128, FF], F32, kind="Internal").ap()

    x1_s = dt("x1_s", [D, T_EXT], F32, kind="ExternalOutput").ap()
    x2_s = dt("x2_s", [D, T_EXT], F32, kind="ExternalOutput").ap()
    x3_s = dt("x3_s", [D, T_OWN], F32, kind="ExternalOutput").ap()
    x4T = dt("x4T", [D, T_OWN], F32, kind="ExternalOutput").ap()

    es = ExitStack()
    with es:
        AR_W = 50900
        ARF_W = 104
        ar_t = es.enter_context(nc.sbuf_tensor("arena", [128, AR_W], F32))
        arr_t = es.enter_context(nc.sbuf_tensor("arenar", [128, ARF_W], F32R))
        ar = Arena(AR_W)
        arf = Arena(ARF_W)
        sm_t = es.enter_context(nc.sbuf_tensor("smallar", [128, 1600], F32))
        small = Arena(1600)
        ones1 = es.enter_context(nc.sbuf_tensor("ones1", [1, 128], F32))
        ps_t = [es.enter_context(nc.psum_tensor(f"ps{i}", [128, 512], F32)) for i in range(8)]
        sems = {e: es.enter_context(nc.semaphore(f"sem_{e}")) for e in ENGS}
        semd = {e: es.enter_context(nc.semaphore(f"semd_{e}")) for e in ENGS}

        def A(off, w, p0=0, pn=128):
            return ar_t[p0:p0 + pn, off:off + w]

        def AR(off, w, p0=0, pn=128):
            return arr_t[p0:p0 + pn, off:off + w]

        def AFR(off, w, p0=0, pn=128):
            return arr_t[p0:p0 + pn, off:off + w].bitcast(F32)

        class Bank:
            def __init__(self):
                self.free_dep = [None] * 8
                self.rr = 0

            def acq(self):
                b = self.rr
                self.rr = (self.rr + 1) % 8
                return b, self.free_dep[b]

            def rel(self, b, dep):
                self.free_dep[b] = dep

        psum = Bank()

        def E(eng, method, *args, deps=(), inc=1, **kw):
            a = tuple(args)
            k = dict(kw)
            return P.emit(eng, (lambda e, m=method, a=a, k=k: getattr(e, m)(*a, **k)),
                          deps=deps, inc=inc)

        def TN():
            return P.now("tensor")

        for name, w in [("tmk", NF), ("tmk1", NF), ("tmr", NF), ("tmr1", NF),
                        ("wdec", NH), ("g0", NF), ("b0", NF), ("us0", NQ),
                        ("ds0", NF), ("os0", 1), ("sug", NF), ("sdp", NQ),
                        ("usg", NQ), ("gsg", NQ), ("uscomb", NQ), ("wsd", NF),
                        ("state", NH), ("ident", 128), ("h128", 128),
                        ("rot", 128), ("w1q", 1), ("w1k", 1), ("onescol", 1),
                        ("tstat", 8), ("dsg", 256)]:
            small.alloc(name, w)

        def SV(name, p0=0, pn=128, w=None):
            o, ww = small.live[name]
            return sm_t[p0:p0 + pn, o:o + (ww if w is None else w)]

        def STAT(i):
            o = small.live["tstat"][0]
            return sm_t[0:128, o + i:o + i + 1]

        def bar():
            return [P.now("tensor"), P.now("vector"), P.now("scalar"),
                    P.now("gpsimd"), P.nowd("scalar"), P.nowd("gpsimd")]


        for nm, src in [("tmk", v_tmk), ("tmr", v_tmr), ("wdec", v_td),
                        ("g0", v_g0), ("b0", v_b0), ("us0", v_us0),
                        ("ds0", v_ds0), ("sug", v_sug), ("sdp", v_sdp),
                        ("usg", v_wsu), ("gsg", v_wsg), ("wsd", v_wsd),
                        ("ident", c_ident), ("h128", c_h128), ("rot", c_rot),
                        ("w1q", v_w1q), ("w1k", v_w1k)]:
            P.dma("sync", SV(nm), src)
        P.dma("sync", SV("os0", 0, 1), v_os0)
        P.dma("sync", ones1[:, :], c_ones1)
        p_mask = ar.alloc("p_mask", NQT * KSPAN)
        p_cosq = ar.alloc("p_cosq", T_OWN)
        p_sinq = ar.alloc("p_sinq", T_OWN)
        p_cosk = ar.alloc("p_cosk", T_EXT)
        p_sink = ar.alloc("p_sink", T_EXT)
        for qt in range(NQT):
            P.dma("sync", A(p_mask + qt * KSPAN, KSPAN), c_mask[qt, :, :])
        P.dma("sync", A(p_cosq, T_OWN), c_cosqT[:, :])
        P.dma("sync", A(p_sinq, T_OWN), c_sinqT[:, :])
        P.dma("sync", A(p_cosk, T_EXT), c_coskT[:, :])
        P.dma("sync", A(p_sink, T_EXT), c_sinkT[:, :])
        dld = P.nowd("sync")

        dv = E("vector", "tensor_scalar", out=SV("tmk1"), in0=SV("tmk"),
               scalar1=-1.0, scalar2=1.0, op0=ALU.mult, op1=ALU.add, deps=[dld])
        dv = E("vector", "tensor_scalar", out=SV("tmr1"), in0=SV("tmr"),
               scalar1=-1.0, scalar2=1.0, op0=ALU.mult, op1=ALU.add, deps=[dv])
        dv = E("vector", "tensor_scalar", out=SV("sug"), in0=SV("sug"),
               scalar1=1.0 / (64.0 * float(np.sqrt(D))), scalar2=None,
               op0=ALU.mult, deps=[dv])
        dv = E("vector", "tensor_scalar", out=SV("sdp"), in0=SV("sdp"),
               scalar1=1.0 / (64.0 * float(np.sqrt(FF))), scalar2=None,
               op0=ALU.mult, deps=[dv])
        dv = E("vector", "tensor_scalar", out=SV("usg"), in0=SV("usg"),
               scalar1=64.0, scalar2=None, op0=ALU.mult, deps=[dv])
        dv = E("vector", "tensor_scalar", out=SV("gsg"), in0=SV("gsg"),
               scalar1=64.0, scalar2=None, op0=ALU.mult, deps=[dv])
        dv = E("vector", "tensor_tensor", out=SV("uscomb"), in0=SV("usg"),
               in1=SV("sdp"), op=ALU.mult, deps=[dv])
        dv = E("vector", "tensor_scalar", out=SV("wsd"), in0=SV("wsd"),
               scalar1=64.0, scalar2=None, op0=ALU.mult, deps=[dv])
        dv = E("vector", "memset", SV("state"), 0.0, deps=[dv])
        dv = E("vector", "memset", SV("onescol"), 1.0, deps=[dv])
        dw = E("scalar", "activation", out=SV("wdec"), in_=SV("wdec"),
               func=AF.Exp, scale=1.0, deps=[dld])
        dw = E("scalar", "activation", out=SV("wdec"), in_=SV("wdec"),
               func=AF.Exp, scale=-1.0, deps=[dw])
        b_os, bd = psum.acq()
        d_os = E("tensor", "matmul", ps_t[b_os][:, 0:1], ones1[:, :],
                 SV("os0", 0, 1), start=True, stop=True,
                 deps=[dld] + ([bd] if bd else []))
        d_osb = E("vector", "tensor_copy", STAT(0), ps_t[b_os][:, 0:1],
                  deps=[d_os, dv])
        psum.rel(b_os, d_osb)
        dv = E("vector", "tensor_scalar", out=SV("ds0"), in0=SV("ds0"),
               scalar1=STAT(0), scalar2=None, op0=ALU.mult, deps=[d_osb])
        DCONST = dv
        DW = dw

        def prep_weight(Wap, dout, din, scr, sign=False):
            nfo, nfi = dout // 128, din // 128
            slab_o = ar.alloc("pw_slab", 2 * din)
            stg_pool = ar
            STG = A
            stg_o = stg_pool.alloc("pw_stg", 2 * din)
            stg_free = [P.nowd("gpsimd"), P.nowd("gpsimd")]
            slab_free = [None, None]
            spill = None
            start_bar = [DCONST, P.now("tensor"), P.now("vector"),
                         P.now("scalar"), P.now("gpsimd"),
                         P.nowd("gpsimd"), P.nowd("scalar")]
            for o in range(nfo):
                sb = o % 2
                dl = P.dma("sync",
                           A(slab_o + sb * din, din),
                           Wap[o * 128:(o + 1) * 128, :],
                           deps=[slab_free[sb]] if slab_free[sb] else start_bar)
                evs = []
                for fg in range(0, nfi, 4):
                    gw = min(4, nfi - fg)
                    b, bdep = psum.acq()
                    mmd = None
                    for j in range(gw):
                        f = fg + j
                        ddd = ([dl, DCONST] + ([bdep] if bdep else [])) if j == 0 else []
                        mmd = E("tensor", "transpose", ps_t[b][:, j * 128:(j + 1) * 128],
                                A(slab_o + sb * din + f * 128, 128), SV("ident"),
                                deps=ddd)
                    sdeps = [TN(), mmd] + ([stg_free[sb]] if stg_free[sb] else [])
                    if sign:
                        ed = E("scalar", "activation",
                               out=STG(stg_o + sb * din + fg * 128, gw * 128),
                               in_=ps_t[b][:, :gw * 128], func=AF.Sign, scale=1.0,
                               deps=sdeps)
                    elif (fg // 4) % 2 == 0:
                        ed = E("vector", "tensor_copy",
                               STG(stg_o + sb * din + fg * 128, gw * 128),
                               ps_t[b][:, :gw * 128], deps=sdeps)
                    else:
                        ed = E("scalar", "copy",
                               out=STG(stg_o + sb * din + fg * 128, gw * 128),
                               in_=ps_t[b][:, :gw * 128], deps=sdeps)
                    psum.rel(b, ed)
                    evs.append(ed)
                spill = P.dma("gpsimd", scr[o, :, :],
                              STG(stg_o + sb * din, din), deps=evs)
                slab_free[sb] = P.now("tensor")
                stg_free[sb] = spill
            ar.freev("pw_slab")
            stg_pool.freev("pw_stg")
            return spill

        prep = {}
        if RUN_A:
            prep["A"] = [prep_weight(Wk0, HID, D, WkT_s),
                         prep_weight(Wr0, HID, D, WrT_s),
                         prep_weight(Wv0, D, HID, WvT_s)]

        def proj(scr, nfo, nki, slab_w, rhs_ap, rhs_dep, tts, evac, wdeps):
            so = ar.alloc("proj_slab", 2 * slab_w)
            sfree = [None, None]
            start_bar = bar()
            odeps = [None] * nfo
            for o in range(nfo):
                sb = o % 2
                dl = P.dma("sync",
                           A(so + sb * slab_w, slab_w), scr[o, :, :],
                           deps=(([sfree[sb]] if sfree[sb] else start_bar)
                                 + list(wdeps)))
                for (t0, tw) in tts:
                    b, bdep = psum.acq()
                    mmd = None
                    for f in range(nki):
                        ddd = ([dl, rhs_dep(f)] + ([bdep] if bdep else [])) if f == 0 \
                            else [rhs_dep(f)]
                        mmd = E("tensor", "matmul", ps_t[b][:, :tw],
                                A(so + sb * slab_w + f * 128, 128),
                                rhs_ap(f, t0, tw), start=(f == 0), stop=(f == nki - 1),
                                deps=ddd)
                    ed = evac(o, t0, tw, ps_t[b], TN())
                    psum.rel(b, ed)
                    odeps[o] = ed
                sfree[sb] = P.now("tensor")
            ar.freev("proj_slab")
            return odeps

        def ln_fm(chunks, w, writer, gamma=None, beta=None):
            bA, dA = psum.acq()
            bB, dB = psum.acq()
            sq_o = ar.alloc("ln_sq", NF * w)
            mm_a = mm_b = None
            sqds = []
            for f in range(NF):
                ap, dep = chunks[f]
                mm_a = E("tensor", "matmul", ps_t[bA][0:1, :w], SV("onescol"), ap,
                         start=(f == 0), stop=(f == NF - 1),
                         deps=[dep, DCONST] + ([dA] if (f == 0 and dA) else []))
            sq_bar = [P.now("tensor"), P.nowd("gpsimd"), P.now("scalar"),
                      P.nowd("scalar")]
            for f in range(NF):
                ap, dep = chunks[f]
                sqds.append(E("vector", "tensor_tensor", out=A(sq_o + f * w, w),
                              in0=ap, in1=ap, op=ALU.mult,
                              deps=[dep] + (sq_bar if f == 0 else [])))
            for f in range(NF):
                mm_b = E("tensor", "matmul", ps_t[bB][0:1, :w], SV("onescol"),
                         A(sq_o + f * w, w), start=(f == 0), stop=(f == NF - 1),
                         deps=[sqds[f]] + ([dB] if (f == 0 and dB) else []))
            prev_sq = mm_b
            ar.freev("ln_sq")
            st_o = ar.alloc("ln_st", 3 * w)
            e1 = A(st_o, w, 0, 1)
            e2 = A(st_o + w, w, 0, 1)
            e3 = A(st_o + 2 * w, w, 0, 1)
            d1 = E("vector", "tensor_scalar", out=e1, in0=ps_t[bA][0:1, :w],
                   scalar1=1.0 / D, scalar2=None, op0=ALU.mult,
                   deps=[TN(), mm_a, prev_sq])
            d2 = E("vector", "tensor_scalar", out=e2, in0=ps_t[bB][0:1, :w],
                   scalar1=1.0 / D, scalar2=EPS_LN, op0=ALU.mult, op1=ALU.add,
                   deps=[mm_b, d1])
            d3 = E("vector", "tensor_tensor", out=e3, in0=e1, in1=e1, op=ALU.mult,
                   deps=[d1, d2])
            d4 = E("vector", "tensor_tensor", out=e2, in0=e2, in1=e3,
                   op=ALU.subtract, deps=[d3])
            d5 = E("vector", "reciprocal", out=e3, in_=e2, deps=[d4])
            d6 = E("scalar", "activation", out=e2, in_=e3, func=AF.Sqrt, scale=1.0,
                   deps=[d5])
            d7 = E("vector", "tensor_tensor", out=e1, in0=e1, in1=e2, op=ALU.mult,
                   deps=[d6])
            psum.rel(bA, d1)
            psum.rel(bB, d2)
            bC, dC = psum.acq()
            bDD, dD = psum.acq()
            dbc = E("tensor", "matmul", ps_t[bC][:, :w], ones1[:, :], e2,
                    start=True, stop=True, deps=[d6, DCONST] + ([dC] if dC else []))
            dbd = E("tensor", "matmul", ps_t[bDD][:, :w], ones1[:, :], e1,
                    start=True, stop=True, deps=[d7] + ([dD] if dD else []))
            dbd = TN()
            ap_o = ar.alloc("ln_ap", w)
            outs = []
            prev_wr = None
            for f in range(NF):
                ap, dep = chunks[f]
                t1 = E("vector", "tensor_tensor", out=A(ap_o, w), in0=ap,
                       in1=ps_t[bC][:, :w], op=ALU.mult,
                       deps=[dbd, dep] + ([prev_wr] if prev_wr else []))
                t2 = E("vector", "tensor_tensor", out=A(ap_o, w), in0=A(ap_o, w),
                       in1=ps_t[bDD][:, :w], op=ALU.subtract, deps=[dbd, t1])
                if gamma is not None:
                    t2 = E("vector", "tensor_scalar", out=A(ap_o, w), in0=A(ap_o, w),
                           scalar1=gamma[0:128, f:f + 1], scalar2=beta[0:128, f:f + 1],
                           op0=ALU.mult, op1=ALU.add, deps=[t2])
                wd = writer(f, A(ap_o, w), t2)
                prev_wr = wd
                outs.append(wd)
            ar.freev("ln_st")
            ar.freev("ln_ap")
            psum.rel(bC, prev_wr)
            psum.rel(bDD, prev_wr)
            return outs

        def build_mix(dst_o, W, ss, tm_name, tm1_name, start_deps=()):
            xb_o = ar.alloc("mix_xb", 1024)
            last = None
            deps = [None] * NF
            first = True
            for f in range(NF):
                for t0 in range(0, W, 448):
                    tw = min(448, W - t0)
                    sd = list(start_deps) if first else []
                    dx = P.dma("sync", A(xb_o, tw),
                               xT[f * 128:(f + 1) * 128, ss + t0: ss + t0 + tw],
                               deps=([last] if last else []) + sd)
                    dxp = P.dma("sync", A(xb_o + 512, tw),
                                xpT[f * 128:(f + 1) * 128, ss + t0: ss + t0 + tw],
                                deps=[last] if last else [])
                    t1 = E("vector", "tensor_scalar", out=A(dst_o + f * W + t0, tw),
                           in0=A(xb_o + 512, tw), scalar1=SV(tm_name)[0:128, f:f + 1],
                           scalar2=None, op0=ALU.mult,
                           deps=[dxp, DCONST] + (sd if first else []))
                    t2 = E("vector", "scalar_tensor_tensor",
                           out=A(dst_o + f * W + t0, tw), in0=A(xb_o, tw),
                           scalar=SV(tm1_name)[0:128, f:f + 1],
                           in1=A(dst_o + f * W + t0, tw),
                           op0=ALU.mult, op1=ALU.add, deps=[dx, t1])
                    last = t2
                    deps[f] = t2
                    first = False
            P.wait("sync", last)
            ar.freev("mix_xb")
            return deps

        depA_done = []
        if RUN_A:
            state_dep = [None] * NH
            halves = [(0, 896, 256, 896), (896, 1792, 896, 1792)]
            for (ss, se, os_, oe) in halves:
                WK = se - ss
                WO = oe - os_
                mixk_o = ar.alloc("mixk", NF * WK)
                mdep = build_mix(mixk_o, WK, ss, "tmk", "tmk1", start_deps=bar())
                k_o = ar.alloc("k", NH * WK)
                tts = [(t0, min(448, WK - t0)) for t0 in range(0, WK, 448)]

                def ev_k(o, t0, tw, ps, mmd, k_o=k_o, WK=WK):
                    return E("scalar", "activation", out=A(k_o + o * WK + t0, tw),
                             in_=ps[:, :tw], func=AF.Relu, scale=1.0, deps=[mmd])
                kdeps = proj(WkT_s, NH, NF, D,
                             (lambda f, t0, tw, mixk_o=mixk_o, WK=WK:
                              A(mixk_o + f * WK + t0, tw)),
                             (lambda f, mdep=mdep: mdep[f]), tts, ev_k, prep["A"])
                ar.freev("mixk")
                for o in range(NH):
                    sd = E("vector", "tensor_tensor_scan", out=A(k_o + o * WK, WK),
                           data0=SV("wdec")[0:128, o:o + 1].broadcast_to((128, WK)),
                           data1=A(k_o + o * WK, WK),
                           initial=SV("state")[0:128, o:o + 1],
                           op0=ALU.mult, op1=ALU.add,
                           deps=[kdeps[o], DW, DCONST] +
                           ([state_dep[o]] if state_dep[o] else []))
                    sd = E("vector", "tensor_copy", SV("state")[0:128, o:o + 1],
                           A(k_o + o * WK + WK - 1, 1), deps=[sd])
                    state_dep[o] = sd
                    kdeps[o] = sd
                mixr_o = ar.alloc("mixr", NF * WO)
                rdep = build_mix(mixr_o, WO, os_, "tmr", "tmr1", start_deps=bar())
                koff = os_ - ss
                tts_o = [(t0, min(448, WO - t0)) for t0 in range(0, WO, 448)]

                def ev_r(o, t0, tw, ps, mmd, k_o=k_o, WK=WK, koff=koff,
                         kdeps=kdeps):
                    return E("vector", "scalar_tensor_tensor",
                             out=A(k_o + o * WK + koff + t0, tw), in0=ps[:, :tw],
                             scalar=1.0, in1=A(k_o + o * WK + koff + t0, tw),
                             op0=ALU.mult, op1=ALU.mult, deps=[mmd, kdeps[o]])
                rdeps = proj(WrT_s, NH, NF, D,
                             (lambda f, t0, tw, mixr_o=mixr_o, WO=WO:
                              A(mixr_o + f * WO + t0, tw)),
                             (lambda f, rdep=rdep: rdep[f]), tts_o, ev_r, prep["A"])
                ar.freev("mixr")
                vout_o = ar.alloc("vout", NF * WO)

                def ev_v(fo, t0, tw, ps, mmd, vout_o=vout_o, WO=WO):
                    return E("vector", "tensor_copy", A(vout_o + fo * WO + t0, tw),
                             ps[:, :tw], deps=[mmd])
                vdeps = proj(WvT_s, NF, NH, HID,
                             (lambda o, t0, tw, k_o=k_o, WK=WK, koff=koff:
                              A(k_o + o * WK + koff + t0, tw)),
                             (lambda o, rdeps=rdeps: rdeps[o]), tts_o, ev_v,
                             prep["A"])
                ar.freev("k")
                for (t0, tw) in tts_o:
                    chunks = [(A(vout_o + f * WO + t0, tw), vdeps[f]) for f in range(NF)]

                    def wr_back(f, ap, dep, vout_o=vout_o, WO=WO, t0=t0, tw=tw):
                        return E("vector", "tensor_copy",
                                 A(vout_o + f * WO + t0, tw), ap, deps=[dep])
                    o1 = ln_fm(chunks, tw, wr_back, gamma=SV("g0"), beta=SV("b0"))
                    xb2_o = ar.alloc("xb2", tw)
                    ch2 = []
                    lastr = None
                    for f in range(NF):
                        dx = P.dma("scalar", A(xb2_o, tw),
                                   xT[f * 128:(f + 1) * 128, os_ + t0: os_ + t0 + tw],
                                   deps=[o1[f]] + ([lastr] if lastr else []))
                        ad = E("vector", "tensor_tensor",
                               out=A(vout_o + f * WO + t0, tw),
                               in0=A(vout_o + f * WO + t0, tw),
                               in1=A(xb2_o, tw), op=ALU.add, deps=[dx, o1[f]])
                        lastr = ad
                        ch2.append((A(vout_o + f * WO + t0, tw), ad))
                    ar.freev("xb2")
                    col0 = (os_ - WARM) + t0

                    def wr_out(f, ap, dep, col0=col0, tw=tw):
                        return P.dma("gpsimd",
                                     x1_s[f * 128:(f + 1) * 128, col0:col0 + tw],
                                     ap, deps=[dep])
                    o2 = ln_fm(ch2, tw, wr_out)
                    depA_done.extend(o2)
                ar.freev("vout")

        depB_done = list(depA_done)
        if RUN_B:
            prep["B"] = [prep_weight(Bup, FF, D, BupT_s, sign=True),
                         prep_weight(Bdn, D, FF, BdnT_s, sign=True)]
            for sw in range(3):
                c0 = sw * 512
                W = 512
                x1_o = ar.alloc("bx1", NF * W)
                x1d = []
                lastl = None
                for f in range(NF):
                    dd = ([lastl] if lastl else []) + (bar() if f == 0 else [])
                    if sw == 0 and f == 0:
                        dd = dd + depA_done
                    dx = P.dma("sync", A(x1_o + f * W, W),
                               x1_s[f * 128:(f + 1) * 128, c0:c0 + W], deps=dd)
                    lastl = dx
                    x1d.append(dx)
                acc_o = ar.alloc("bacc", NF * W)
                accd = [None] * NF
                h_o = ar.alloc("bh", 8 * W)
                up_so = ar.alloc("bup_slab", 2 * D)
                dn_so = ar.alloc("bdn_slab", 2 * 1024)
                upfree = [None, None]
                dnfree = [None, None]
                for slab in range(8):
                    hdeps = [None] * 8
                    for ho in range(8):
                        o = slab * 8 + ho
                        sb = ho % 2
                        dl = P.dma("sync",
                                   A(up_so + sb * D, D), BupT_s[o, :, :],
                                   deps=([upfree[sb]] if upfree[sb] else []) + prep["B"])
                        b, bdep = psum.acq()
                        mmd = None
                        for f in range(NF):
                            ddd = ([dl, x1d[f]] + ([bdep] if bdep else [])) if f == 0 \
                                else [x1d[f]]
                            mmd = E("tensor", "matmul", ps_t[b][:, :W],
                                    A(up_so + sb * D + f * 128, 128),
                                    A(x1_o + f * W, W), start=(f == 0),
                                    stop=(f == NF - 1), deps=ddd)
                        ed = E("scalar", "activation", out=A(h_o + ho * W, W),
                               in_=ps_t[b][:, :W], func=AF.Relu,
                               scale=SV("us0")[0:128, o:o + 1],
                               deps=[mmd, DCONST])
                        psum.rel(b, ed)
                        hdeps[ho] = ed
                        upfree[sb] = P.now("tensor")
                    for fo in range(NF):
                        sb = fo % 2
                        dl = P.dma("sync",
                                   A(dn_so + sb * 1024, 1024),
                                   BdnT_s[fo, :, slab * 1024:(slab + 1) * 1024],
                                   deps=([dnfree[sb]] if dnfree[sb] else []) + prep["B"])
                        b, bdep = psum.acq()
                        mmd = None
                        for ho in range(8):
                            ddd = ([dl, hdeps[ho]] + ([bdep] if bdep else [])) if ho == 0 \
                                else [hdeps[ho]]
                            mmd = E("tensor", "matmul", ps_t[b][:, :W],
                                    A(dn_so + sb * 1024 + ho * 128, 128),
                                    A(h_o + ho * W, W), start=(ho == 0),
                                    stop=(ho == 7), deps=ddd)
                        if slab == 0:
                            ed = E("vector", "tensor_copy", A(acc_o + fo * W, W),
                                   ps_t[b][:, :W], deps=[mmd])
                        else:
                            ed = E("vector", "tensor_tensor", out=A(acc_o + fo * W, W),
                                   in0=A(acc_o + fo * W, W), in1=ps_t[b][:, :W],
                                   op=ALU.add, deps=[mmd, accd[fo]])
                        psum.rel(b, ed)
                        accd[fo] = ed
                        dnfree[sb] = P.now("tensor")
                ar.freev("bh")
                ar.freev("bup_slab")
                ar.freev("bdn_slab")
                ch2 = []
                for f in range(NF):
                    sd = E("vector", "tensor_scalar", out=A(acc_o + f * W, W),
                           in0=A(acc_o + f * W, W), scalar1=SV("ds0")[0:128, f:f + 1],
                           scalar2=None, op0=ALU.mult, deps=[accd[f], DCONST])
                    ad = E("vector", "tensor_tensor", out=A(acc_o + f * W, W),
                           in0=A(acc_o + f * W, W), in1=A(x1_o + f * W, W),
                           op=ALU.add, deps=[sd, x1d[f]])
                    ch2.append((A(acc_o + f * W, W), ad))

                def wr_x2(f, ap, dep, c0=c0, W=W):
                    return P.dma("gpsimd", x2_s[f * 128:(f + 1) * 128, c0:c0 + W],
                                 ap, deps=[dep])
                o2 = ln_fm(ch2, W, wr_x2)
                depB_done.extend(o2)
                ar.freev("bx1")
                ar.freev("bacc")

        depC_done = list(depB_done)
        if RUN_C:
            prep["C"] = [prep_weight(Wq1, D, D, WqT_s),
                         prep_weight(Wk1, KVH * DH, D, WkT1_s),
                         prep_weight(Wv1, KVH * DH, D, WvT1_s),
                         prep_weight(Wo1, D, D, WoT_s)]
            wC = prep["C"]
            kT_o = ar.alloc("kT", KVH * T_EXT)
            vT_o = ar.alloc("vT", KVH * T_EXT)
            x2b_o = ar.alloc("x2b", NF * 512)
            kslab_o = ar.alloc("kv_slab", 2 * D)
            ksfree = [None, None]
            kdeps1 = [None] * KVH
            vdeps1 = [None] * KVH
            for (t0, tw) in [(0, 512), (512, 512), (1024, 512)]:
                x2d = []
                lastl = None
                for f in range(NF):
                    dd = ([lastl] if lastl else []) + (bar() if f == 0 else [])
                    if t0 == 0 and f == 0:
                        dd = dd + depB_done
                    dx = P.dma("sync", A(x2b_o + f * 512, tw),
                               x2_s[f * 128:(f + 1) * 128, t0:t0 + tw], deps=dd)
                    lastl = dx
                    x2d.append(dx)
                for scr, dst_o, dst_deps in [(WkT1_s, kT_o, kdeps1),
                                             (WvT1_s, vT_o, vdeps1)]:
                    for g in range(KVH):
                        sb = g % 2
                        dl = P.dma("sync",
                                   A(kslab_o + sb * D, D), scr[g, :, :],
                                   deps=([ksfree[sb]] if ksfree[sb] else []) + wC)
                        b, bdep = psum.acq()
                        mmd = None
                        for f in range(NF):
                            ddd = ([dl, x2d[f]] + ([bdep] if bdep else [])) if f == 0 \
                                else [x2d[f]]
                            mmd = E("tensor", "matmul", ps_t[b][:, :tw],
                                    A(kslab_o + sb * D + f * 128, 128),
                                    A(x2b_o + f * 512, tw), start=(f == 0),
                                    stop=(f == NF - 1), deps=ddd)
                        ed = E("vector", "tensor_copy",
                               A(dst_o + g * T_EXT + t0, tw), ps_t[b][:, :tw],
                               deps=[TN()])
                        psum.rel(b, ed)
                        dst_deps[g] = ed
                        ksfree[sb] = P.now("tensor")
            ar.freev("x2b")
            ar.freev("kv_slab")

            def rms_rope_fm(base_o, W, w1_ap, cos_o, sin_o, fold, dep_in):
                sq_o = ar.alloc("rr_sq", 512)
                st_o = ar.alloc("rr_st", 1024)
                rb_o = ar.alloc("rr_rb", 512)
                last = dep_in
                for t0 in range(0, W, 512):
                    tw = min(512, W - t0)
                    ap = A(base_o + t0, tw)
                    bS, dS = psum.acq()
                    sqd = E("vector", "tensor_tensor", out=A(sq_o, tw), in0=ap,
                            in1=ap, op=ALU.mult, deps=last)
                    mms = E("tensor", "matmul", ps_t[bS][0:1, :tw], SV("onescol"),
                            A(sq_o, tw), start=True, stop=True,
                            deps=[sqd, DCONST] + ([dS] if dS else []))
                    s1 = A(st_o, tw, 0, 1)
                    s2 = A(st_o + 512, tw, 0, 1)
                    d1 = E("vector", "tensor_scalar", out=s1, in0=ps_t[bS][0:1, :tw],
                           scalar1=1.0 / DH, scalar2=EPS_RMS, op0=ALU.mult,
                           op1=ALU.add, deps=[TN(), mms])
                    psum.rel(bS, d1)
                    d2 = E("vector", "reciprocal", out=s2, in_=s1, deps=[d1])
                    d3 = E("scalar", "activation", out=s1, in_=s2, func=AF.Sqrt,
                           scale=fold, deps=[d2])
                    bB, dB2 = psum.acq()
                    dbc = E("tensor", "matmul", ps_t[bB][:, :tw], ones1[:, :], s1,
                            start=True, stop=True, deps=[d3] + ([dB2] if dB2 else []))
                    n1 = E("vector", "tensor_tensor", out=ap, in0=ap,
                           in1=ps_t[bB][:, :tw], op=ALU.mult, deps=[TN(), dbc])
                    psum.rel(bB, n1)
                    n2 = E("vector", "tensor_scalar", out=ap, in0=ap, scalar1=w1_ap,
                           scalar2=None, op0=ALU.mult, deps=[n1, DCONST])
                    bR, dR = psum.acq()
                    mr = E("tensor", "matmul", ps_t[bR][:, :tw], SV("rot"), ap,
                           start=True, stop=True,
                           deps=[n2, DCONST] + ([dR] if dR else []))
                    mr = TN()
                    r1 = E("vector", "tensor_tensor", out=A(rb_o, tw), in0=ap,
                           in1=A(cos_o + t0, tw), op=ALU.mult, deps=[mr])
                    r2 = E("vector", "scalar_tensor_tensor", out=ap,
                           in0=ps_t[bR][:, :tw], scalar=1.0,
                           in1=A(sin_o + t0, tw), op0=ALU.mult, op1=ALU.mult,
                           deps=[r1])
                    r3 = E("vector", "tensor_tensor", out=ap, in0=ap,
                           in1=A(rb_o, tw), op=ALU.add, deps=[r2])
                    psum.rel(bR, r3)
                    last = [r3]
                ar.freev("rr_sq")
                ar.freev("rr_st")
                ar.freev("rr_rb")
                return last[0]

            rms_bar = bar()
            for g in range(KVH):
                kdeps1[g] = rms_rope_fm(
                    kT_o + g * T_EXT, T_EXT, SV("w1k")[0:128, 0:1],
                    p_cosk, p_sink, 1.0, [kdeps1[g]] + rms_bar)
            vtm_o = ar.alloc("vtm", NKT * 512)
            vtmd = [None] * NKT
            for kt in range(NKT):
                for g in range(KVH):
                    b, bdep = psum.acq()
                    md = E("tensor", "transpose", ps_t[b][:, 0:128],
                           A(vT_o + g * T_EXT + kt * 128, 128), SV("ident"),
                           deps=[vdeps1[g], DCONST] + ([bdep] if bdep else []))
                    ed = E("vector", "tensor_copy",
                           A(vtm_o + kt * 512 + g * 128, 128), ps_t[b][:, 0:128],
                           deps=[TN()])
                    psum.rel(b, ed)
                    vtmd[kt] = ed
            ar.freev("vT")

            x3deps = []
            laste0 = None
            lastctx = None
            for grp in range(NQT // 2):
                q0 = grp * 256
                qT_o = ar.alloc("qT", HEADS * 256)
                x2q_o = ar.alloc("x2q", NF * 256)
                qd = []
                lastl = None
                for f in range(NF):
                    dd = ([lastl] if lastl else []) + (bar() if f == 0 else [])
                    dx = P.dma("sync", A(x2q_o + f * 256, 256),
                               x2_s[f * 128:(f + 1) * 128, HALO + q0: HALO + q0 + 256],
                               deps=dd)
                    lastl = dx
                    qd.append(dx)

                def ev_q(j, t0, tw, ps, mmd, qT_o=qT_o):
                    return E("vector", "tensor_copy", A(qT_o + j * 256, tw),
                             ps[:, :tw], deps=[mmd])
                qdeps = proj(WqT_s, HEADS, NF, D,
                             (lambda f, t0, tw, x2q_o=x2q_o: A(x2q_o + f * 256, tw)),
                             (lambda f, qd=qd: qd[f]), [(0, 256)], ev_q, wC)
                for j in range(HEADS):
                    qdeps[j] = rms_rope_fm(
                        qT_o + j * 256, 256, SV("w1q")[0:128, 0:1],
                        p_cosq + q0, p_sinq + q0,
                        1.0 / float(DH), [qdeps[j]])
                ar.freev("x2q")
                ctx_o = ar.alloc("ctx", HEADS * 256)
                e0_o = ar.alloc("att_e0", KSPAN)
                at_o = ar.alloc("att_t", KSPAN)
                ctxdeps = [None] * HEADS
                last_exp = None
                for j in range(HEADS):
                    g = j // G
                    for qi in range(2):
                        qt = grp * 2 + qi
                        bA_, dA_ = psum.acq()
                        bB_, dB_ = psum.acq()
                        m1 = E("tensor", "matmul", ps_t[bA_][:, :512],
                               A(qT_o + j * 256 + qi * 128, 128),
                               A(kT_o + g * T_EXT + qt * 128, 512),
                               start=True, stop=True,
                               deps=[qdeps[j], kdeps1[g]] + ([dA_] if dA_ else []))
                        m2 = E("tensor", "matmul", ps_t[bB_][:, :128],
                               A(qT_o + j * 256 + qi * 128, 128),
                               A(kT_o + g * T_EXT + qt * 128 + 512, 128),
                               start=True, stop=True, deps=[dB_] if dB_ else [])
                        m2f = TN()
                        a1 = E("vector", "tensor_tensor", out=ps_t[bA_][:, :512],
                               in0=ps_t[bA_][:, :512],
                               in1=A(p_mask + qt * KSPAN, 512), op=ALU.add,
                               deps=[m2f, m1, DCONST])
                        a2 = E("vector", "tensor_tensor", out=ps_t[bB_][:, :128],
                               in0=ps_t[bB_][:, :128],
                               in1=A(p_mask + qt * KSPAN + 512, 128), op=ALU.add,
                               deps=[m2, a1])
                        nm1, nm2, sa, sb_ = STAT(1), STAT(2), STAT(3), STAT(4)
                        r1 = E("vector", "tensor_reduce", nm1, ps_t[bA_][:, :512],
                               axis=AX.X, op=ALU.max, negate=True,
                               deps=[a1] + ([last_exp] if last_exp else []))
                        r2 = E("vector", "tensor_reduce", nm2, ps_t[bB_][:, :128],
                               axis=AX.X, op=ALU.max, negate=True, deps=[a2, r1])
                        r3 = E("vector", "tensor_tensor", out=nm1, in0=nm1, in1=nm2,
                               op=ALU.min, deps=[r2])
                        x1_ = E("scalar", "activation", out=A(e0_o, 512),
                                in_=ps_t[bA_][:, :512], func=AF.Exp, bias=nm1,
                                scale=1.0, accum_out=sa,
                                deps=[r3] + ([laste0] if laste0 else []))
                        x2_ = E("scalar", "activation", out=A(e0_o + 512, 128),
                                in_=ps_t[bB_][:, :128], func=AF.Exp, bias=nm1,
                                scale=1.0, accum_out=sb_, deps=[x1_])
                        last_exp = x2_
                        psum.rel(bA_, x1_)
                        psum.rel(bB_, x2_)
                        s1 = E("vector", "tensor_tensor", out=sa, in0=sa, in1=sb_,
                               op=ALU.add, deps=[x2_])
                        s2 = E("vector", "reciprocal", out=sb_, in_=sa, deps=[s1])
                        n1 = E("vector", "tensor_scalar", out=A(e0_o, KSPAN),
                               in0=A(e0_o, KSPAN), scalar1=sb_, scalar2=None,
                               op0=ALU.mult, deps=[s2])
                        tds = []
                        for kb in range(5):
                            bT_, dT_ = psum.acq()
                            td = E("tensor", "transpose", ps_t[bT_][:, 0:128],
                                   A(e0_o + kb * 128, 128), SV("ident"),
                                   deps=[n1, DCONST] + ([dT_] if dT_ else []))
                            te = E("vector", "tensor_copy", A(at_o + kb * 128, 128),
                                   ps_t[bT_][:, 0:128],
                                   deps=[TN()] + ([lastctx] if lastctx else []))
                            psum.rel(bT_, te)
                            tds.append(te)
                        laste0 = tds[-1]
                        bC_, dC_ = psum.acq()
                        ctxd = None
                        for kb in range(5):
                            kt = qt + kb
                            ddd = [tds[kb], vtmd[kt], DCONST]
                            if kb == 0 and dC_:
                                ddd.append(dC_)
                            ctxd = E("tensor", "matmul", ps_t[bC_][:, :128],
                                     A(vtm_o + kt * 512 + g * 128, 128),
                                     A(at_o + kb * 128, 128),
                                     start=(kb == 0), stop=(kb == 4), deps=ddd)
                        ce = E("vector", "tensor_copy",
                               A(ctx_o + j * 256 + qi * 128, 128),
                               ps_t[bC_][:, :128], deps=[TN()])
                        psum.rel(bC_, ce)
                        lastctx = ctxd
                        ctxdeps[j] = ce
                ar.freev("att_e0")
                ar.freev("att_t")
                ar.freev("qT")
                out_o = ar.alloc("cout", NF * 256)

                def ev_o(fo, t0, tw, ps, mmd, out_o=out_o):
                    return E("vector", "tensor_copy", A(out_o + fo * 256, tw),
                             ps[:, :tw], deps=[mmd])
                odeps = proj(WoT_s, NF, HEADS, D,
                             (lambda j, t0, tw, ctx_o=ctx_o: A(ctx_o + j * 256, tw)),
                             (lambda j, ctxdeps=ctxdeps: ctxdeps[j]),
                             [(0, 256)], ev_o, wC)
                ar.freev("ctx")
                xr_o = ar.alloc("cxr", 256)
                ch2 = []
                lastx = None
                for f in range(NF):
                    dx = P.dma("scalar", A(xr_o, 256),
                               x2_s[f * 128:(f + 1) * 128, HALO + q0: HALO + q0 + 256],
                               deps=[odeps[f]] + ([lastx] if lastx else []))
                    ad = E("vector", "tensor_tensor", out=A(out_o + f * 256, 256),
                           in0=A(out_o + f * 256, 256), in1=A(xr_o, 256),
                           op=ALU.add, deps=[dx, odeps[f]])
                    lastx = ad
                    ch2.append((A(out_o + f * 256, 256), ad))
                ar.freev("cxr")

                def wr_x3(f, ap, dep, q0=q0):
                    return P.dma("gpsimd", x3_s[f * 128:(f + 1) * 128, q0:q0 + 256],
                                 ap, deps=[dep])
                o3 = ln_fm(ch2, 256, wr_x3)
                x3deps.extend(o3)
                ar.freev("cout")
            depC_done = x3deps
            ar.freev("kT")
            ar.freev("vtm")

        if RUN_D:
            prep["D"] = [prep_weight(Uup, FF, D, UupT_s),
                         prep_weight(Ug, FF, D, UgT_s),
                         prep_weight(Udn, D, FF, UdnT_s)]
            wD = prep["D"]
            TC = 256
            for sw in range(T_OWN // TC):
                c0 = sw * TC
                x3_o = ar.alloc("dx3", NF * TC)
                xs_o = ar.alloc("dxs", NF * TC)
                x3d = []
                lastl = None
                for f in range(NF):
                    dd = ([lastl] if lastl else []) + (bar() if f == 0 else [])
                    if sw == 0 and f == 0:
                        dd = dd + depC_done
                    dx = P.dma("sync", A(x3_o + f * TC, TC),
                               x3_s[f * 128:(f + 1) * 128, c0:c0 + TC], deps=dd)
                    lastl = dx
                    x3d.append(dx)
                xsd = []
                for f in range(NF):
                    sd = E("vector", "tensor_scalar", out=A(xs_o + f * TC, TC),
                           in0=A(x3_o + f * TC, TC),
                           scalar1=SV("sug")[0:128, f:f + 1], scalar2=None,
                           op0=ALU.mult, deps=[x3d[f], DCONST])
                    xsd.append(sd)

                def fwht(base_o, n_chunks, deps_list, TC=TC):
                    nst = {16: 4, 64: 6}[n_chunks]
                    for st in range(nst):
                        s = 1 << st
                        ndep = list(deps_list)
                        for i in range(n_chunks):
                            if i & s:
                                continue
                            a_ap = A(base_o + i * TC, TC)
                            b_ap = A(base_o + (i | s) * TC, TC)
                            d1 = E("vector", "tensor_tensor", out=b_ap, in0=a_ap,
                                   in1=b_ap, op=ALU.subtract,
                                   deps=[deps_list[i], deps_list[i | s]])
                            d2 = E("vector", "scalar_tensor_tensor", out=a_ap,
                                   in0=a_ap, scalar=2.0, in1=b_ap,
                                   op0=ALU.mult, op1=ALU.subtract, deps=[d1])
                            ndep[i] = d2
                            ndep[i | s] = d2
                        deps_list = ndep
                    out = []
                    for i in range(n_chunks):
                        b, bdep = psum.acq()
                        md = E("tensor", "matmul", ps_t[b][:, :TC], SV("h128"),
                               A(base_o + i * TC, TC), start=True, stop=True,
                               deps=[deps_list[i], DCONST] + ([bdep] if bdep else []))
                        ed = E("vector", "tensor_copy", A(base_o + i * TC, TC),
                               ps_t[b][:, :TC], deps=[TN(), md])
                        psum.rel(b, ed)
                        out.append(ed)
                    return out

                xsd = fwht(xs_o, NF, xsd)
                z_o = ar.alloc("dz", NQ * TC)
                us_o = ar.alloc("dup_slab", 2 * D)
                gs_o = ar.alloc("dg_slab", 2 * D)
                usfree = [None, None]
                gsfree = [None, None]
                zdeps = [None] * NQ
                prev_sg = None
                for o in range(NQ):
                    sb = o % 2
                    qe = "sync"
                    dlu = P.dma(qe, A(us_o + sb * D, D), UupT_s[o, :, :],
                                deps=([usfree[sb]] if usfree[sb] else []) + wD)
                    dlg = P.dma("sync", A(gs_o + sb * D, D), UgT_s[o, :, :],
                                deps=([gsfree[sb]] if gsfree[sb] else []) + wD)
                    bu, bud = psum.acq()
                    bg, bgd = psum.acq()
                    mmu = mmg = None
                    for f in range(NF):
                        du = ([dlu, xsd[f]] + ([bud] if bud else [])) if f == 0 else [xsd[f]]
                        mmu = E("tensor", "matmul", ps_t[bu][:, :TC],
                                A(us_o + sb * D + f * 128, 128), A(xs_o + f * TC, TC),
                                start=(f == 0), stop=(f == NF - 1), deps=du)
                        dg = ([dlg] + ([bgd] if bgd else [])) if f == 0 else []
                        mmg = E("tensor", "matmul", ps_t[bg][:, :TC],
                                A(gs_o + sb * D + f * 128, 128), A(xs_o + f * TC, TC),
                                start=(f == 0), stop=(f == NF - 1), deps=dg)
                    mmf = TN()
                    sgd = E("scalar", "activation", out=SV("dsg"),
                            in_=ps_t[bg][:, :TC], func=AF.Silu,
                            scale=SV("gsg")[0:128, o:o + 1],
                            deps=[mmf, mmg, DCONST] + ([prev_sg] if prev_sg else []))
                    zd = E("vector", "scalar_tensor_tensor", out=A(z_o + o * TC, TC),
                           in0=ps_t[bu][:, :TC], scalar=SV("uscomb")[0:128, o:o + 1],
                           in1=SV("dsg"), op0=ALU.mult, op1=ALU.mult,
                           deps=[mmf, sgd])
                    prev_sg = zd
                    psum.rel(bu, zd)
                    psum.rel(bg, sgd)
                    zdeps[o] = zd
                    usfree[sb] = P.now("tensor")
                    gsfree[sb] = P.now("tensor")
                ar.freev("dup_slab")
                ar.freev("dg_slab")
                ar.freev("dxs")
                zdeps = fwht(z_o, NQ, zdeps)
                dn_o = ar.alloc("ddn_slab", 2 * 1024)
                dn_bar = bar()
                dnfree = [None, None]
                out_o = ar.alloc("dout", NF * TC)
                outd = [None] * NF
                for fo in range(NF):
                    b, bdep = psum.acq()
                    accmm = None
                    for og in range(8):
                        sb = og % 2
                        dl = P.dma("sync",
                                   A(dn_o + sb * 1024, 1024),
                                   UdnT_s[fo, :, og * 1024:(og + 1) * 1024],
                                   deps=([dnfree[sb]] if dnfree[sb] else dn_bar) + wD)
                        for hh in range(8):
                            o = og * 8 + hh
                            ddd = [dl, zdeps[o]]
                            if o == 0 and bdep:
                                ddd.append(bdep)
                            accmm = E("tensor", "matmul", ps_t[b][:, :TC],
                                      A(dn_o + sb * 1024 + hh * 128, 128),
                                      A(z_o + o * TC, TC), start=(o == 0),
                                      stop=(o == NQ - 1), deps=ddd)
                        dnfree[sb] = P.now("tensor")
                    sd = E("vector", "tensor_scalar", out=A(out_o + fo * TC, TC),
                           in0=ps_t[b][:, :TC], scalar1=SV("wsd")[0:128, fo:fo + 1],
                           scalar2=None, op0=ALU.mult, deps=[TN(), accmm, DCONST])
                    psum.rel(b, sd)
                    ad = E("vector", "tensor_tensor", out=A(out_o + fo * TC, TC),
                           in0=A(out_o + fo * TC, TC), in1=A(x3_o + fo * TC, TC),
                           op=ALU.add, deps=[sd, x3d[fo]])
                    outd[fo] = ad
                ar.freev("ddn_slab")
                ar.freev("dz")
                ch = [(A(out_o + f * TC, TC), outd[f]) for f in range(NF)]

                def wr_x4(f, ap, dep, c0=c0, TC=TC):
                    return P.dma("gpsimd", x4T[f * 128:(f + 1) * 128, c0:c0 + TC],
                                 ap, deps=[dep])
                ln_fm(ch, TC, wr_x4)
                ar.freev("dout")
                ar.freev("dx3")

        P.wait("gpsimd", P.nowd("gpsimd"))
        P.wait("gpsimd", P.nowd("sync"))
        P.wait("gpsimd", P.nowd("scalar"))
        P.wait("gpsimd", P.now("gpsimd"))
        P.wait("scalar", P.now("scalar"))

        with nc.Block() as block:
            def mk(eng_name):
                def body(e):
                    for item in P.ops[eng_name]:
                        if item[0] == "wait":
                            e.wait_ge(semd[item[1]] if item[2] else sems[item[1]],
                                      item[3])
                        else:
                            item[1](e).then_inc(
                                semd[eng_name] if item[2] == 16 else sems[eng_name],
                                item[2])
                return body
            block.tensor(mk("tensor"))
            block.vector(mk("vector"))
            block.scalar(mk("scalar"))
            block.gpsimd(mk("gpsimd"))
            block.sync(mk("sync"))
    return nc


def _consts():
    ident = np.eye(128, dtype=np.float32)
    h = np.array([[1.0]], dtype=np.float32)
    while h.shape[0] < 128:
        h = np.block([[h, h], [h, -h]])
    rotT = np.zeros((128, 128), dtype=np.float32)
    for p in range(64):
        rotT[p + 64, p] = -1.0
        rotT[p, p + 64] = 1.0
    ones1 = np.ones((1, 128), dtype=np.float32)
    return ident, np.ascontiguousarray(h, dtype=np.float32), rotT, ones1


def _rope_tables():
    inv = 1.0 / (10000.0 ** (np.arange(0, DH, 2, dtype=np.float64) / DH))
    pos = np.arange(TSEQ, dtype=np.float64)
    fr = pos[:, None] * inv[None, :]
    emb = np.concatenate([fr, fr], axis=-1)
    return np.cos(emb).astype(np.float32), np.sin(emb).astype(np.float32)


def _mask_for_core(c):
    m = np.zeros((NQT, 128, KSPAN), dtype=np.float32)
    NEG = np.float32(np.finfo(np.float32).min)
    jj = np.arange(KSPAN)
    for qt in range(NQT):
        for i in range(128):
            valid = (jj >= i) & (jj <= i + WINDOW)
            if c == 0:
                valid &= (qt * 128 + jj) >= HALO
            m[qt, i, :] = np.where(valid, 0.0, NEG)
    return m


def _vec_rs(v, nch):
    return np.ascontiguousarray(np.asarray(v, dtype=np.float32).reshape(nch, 128).T)


def kernel(**inputs):
    x = np.asarray(inputs["L0_attn_x"], dtype=np.float32)[0]
    xT_full = np.ascontiguousarray(x.T)
    xp = np.concatenate([x[1:2], x[:-1]], axis=0)
    xpT_full = np.ascontiguousarray(xp.T)
    ident, h128, rotT, ones1 = _consts()
    cos_t, sin_t = _rope_tables()

    common = dict(
        Wk0=np.ascontiguousarray(inputs["L0_attn_Wk"], dtype=np.float32),
        Wr0=np.ascontiguousarray(inputs["L0_attn_Wr"], dtype=np.float32),
        Wv0=np.ascontiguousarray(inputs["L0_attn_Wv"], dtype=np.float32),
        Bup=np.ascontiguousarray(inputs["L0_ffn_up_weight"], dtype=np.float32),
        Bdn=np.ascontiguousarray(inputs["L0_ffn_down_weight"], dtype=np.float32),
        Wq1=np.ascontiguousarray(inputs["L1_attn_Wq"], dtype=np.float32),
        Wk1=np.ascontiguousarray(inputs["L1_attn_Wk"], dtype=np.float32),
        Wv1=np.ascontiguousarray(inputs["L1_attn_Wv"], dtype=np.float32),
        Wo1=np.ascontiguousarray(inputs["L1_attn_Wo"], dtype=np.float32),
        Uup=np.ascontiguousarray(inputs["L1_ffn_W_up"], dtype=np.float32),
        Ug=np.ascontiguousarray(inputs["L1_ffn_W_gate"], dtype=np.float32),
        Udn=np.ascontiguousarray(inputs["L1_ffn_W_down"], dtype=np.float32),
        v_tmk=_vec_rs(inputs["L0_attn_time_mix_k"], NF),
        v_tmr=_vec_rs(inputs["L0_attn_time_mix_r"], NF),
        v_td=_vec_rs(inputs["L0_attn_time_decay"], NH),
        v_g0=_vec_rs(inputs["L0_attn_ln_gamma"], NF),
        v_b0=_vec_rs(inputs["L0_attn_ln_beta"], NF),
        v_us0=_vec_rs(inputs["L0_ffn_up_scale"], NQ),
        v_ds0=_vec_rs(inputs["L0_ffn_down_scale"], NF),
        v_os0=np.ascontiguousarray(np.asarray(inputs["L0_ffn_output_scale"],
                                              dtype=np.float32).reshape(1, 1)),
        v_sug=_vec_rs(inputs["L1_ffn_SU_ug"], NF),
        v_sdp=_vec_rs(inputs["L1_ffn_SU_dp"], NQ),
        v_wsu=_vec_rs(np.asarray(inputs["L1_ffn_Wscale_ug"])[:FF], NQ),
        v_wsg=_vec_rs(np.asarray(inputs["L1_ffn_Wscale_ug"])[FF:], NQ),
        v_wsd=_vec_rs(inputs["L1_ffn_Wscale_dp"], NF),
        v_w1q=np.ascontiguousarray(
            (1.0 + np.asarray(inputs["L1_attn_qn_w"], dtype=np.float32)).reshape(128, 1)),
        v_w1k=np.ascontiguousarray(
            (1.0 + np.asarray(inputs["L1_attn_kn_w"], dtype=np.float32)).reshape(128, 1)),
        c_ident=ident, c_h128=h128, c_rot=rotT, c_ones1=ones1,
    )

    in_maps = []
    for c in range(NCORES):
        t0 = c * T_OWN
        lo = t0 - (HALO + WARM)
        sl_x = np.zeros((D, T_SCAN), dtype=np.float32)
        sl_xp = np.zeros((D, T_SCAN), dtype=np.float32)
        s0 = max(0, lo)
        sl_x[:, s0 - lo:] = xT_full[:, s0:t0 + T_OWN]
        sl_xp[:, s0 - lo:] = xpT_full[:, s0:t0 + T_OWN]
        ke = t0 - HALO
        cosk = np.zeros((T_EXT, DH), dtype=np.float32)
        sink = np.zeros((T_EXT, DH), dtype=np.float32)
        k0 = max(0, ke)
        cosk[k0 - ke:, :] = cos_t[k0:t0 + T_OWN]
        sink[k0 - ke:, :] = sin_t[k0:t0 + T_OWN]
        m = dict(common)
        m.update(xT=sl_x, xpT=sl_xp,
                 c_cosqT=np.ascontiguousarray(cos_t[t0:t0 + T_OWN].T),
                 c_sinqT=np.ascontiguousarray(sin_t[t0:t0 + T_OWN].T),
                 c_coskT=np.ascontiguousarray(cosk.T),
                 c_sinkT=np.ascontiguousarray(sink.T),
                 c_mask=_mask_for_core(c))
        in_maps.append(m)

    nc = build_program()
    res = run_bass_kernel_spmd(nc, in_maps, core_ids=list(range(NCORES)))
    shards = [np.asarray(r["x4T"]) for r in res.results]
    out = np.concatenate(shards, axis=1).T.reshape(1, TSEQ, D)
    return np.ascontiguousarray(out).astype(np.float32)
